# version 07 (final)
# baseline (speedup 1.0000x reference)
import contextlib
import math
import os
import numpy as np
import ml_dtypes

import concourse.bass as bass
import concourse.mybir as mybir
from concourse.bass_utils import run_bass_kernel_spmd

F32 = mybir.dt.float32
BF16 = mybir.dt.bfloat16
U8 = mybir.dt.uint8
AF = mybir.ActivationFunctionType
OP = mybir.AluOpType
AX = mybir.AxisListType.X

D = 4096
B = 4
S = 4096
R = 2048
NDC = D // 128
NTT = R // 128
NTG = R // 512
NCORES = 8
ATT_SCALE = 1.0 / math.sqrt(128.0)
EPS = 1e-5

GROUPS_EVEN = [0, 3, 4, 7]
GROUPS_ODD = [1, 2, 5, 6]
BOUND = [8, 16, 24, 32]
CAND = [list(range(b - 8, b)) for b in BOUND]

BF = ml_dtypes.bfloat16


def groups_for_core(c):
    return GROUPS_EVEN if c % 2 == 0 else GROUPS_ODD


def rows_for_core(c):
    b = c // 2
    return np.concatenate(
        [b * S + 512 * g + np.arange(512) for g in groups_for_core(c)]
    )



def tile_lhsT(W):
    K, M = W.shape
    return np.ascontiguousarray(
        W.reshape(K // 128, 128, M // 128, 128).transpose(2, 1, 0, 3)
    )


def tile_rhs(Wr, n):
    K = Wr.shape[0]
    return np.ascontiguousarray(Wr.reshape(K // 128, 128, n).transpose(1, 0, 2))


def qkv_perm():
    perm = []
    for p in range(16):
        h0, h1 = 2 * p, 2 * p + 1
        perm.extend(h0 * 128 + np.arange(0, 128, 2))
        perm.extend(h1 * 128 + np.arange(0, 128, 2))
        perm.extend(h0 * 128 + np.arange(1, 128, 2))
        perm.extend(h1 * 128 + np.arange(1, 128, 2))
    for p in range(4):
        h0, h1 = 2 * p, 2 * p + 1
        perm.extend(4096 + h0 * 128 + np.arange(0, 128, 2))
        perm.extend(4096 + h1 * 128 + np.arange(0, 128, 2))
        perm.extend(4096 + h0 * 128 + np.arange(1, 128, 2))
        perm.extend(4096 + h1 * 128 + np.arange(1, 128, 2))
    perm.extend(4096 + 1024 + np.arange(1024))
    return np.asarray(perm)


def rope_tables(c):
    pos = (rows_for_core(c) % S).astype(np.float64)
    inv_freq = 1.0 / (10000.0 ** (np.arange(0, 128, 2, dtype=np.float64) / 128.0))
    ang = inv_freq[:, None] * pos[None, :]
    cos = np.cos(ang).astype(np.float32)
    sin = np.sin(ang).astype(np.float32)
    return (
        np.ascontiguousarray(np.concatenate([cos, cos], axis=0)),
        np.ascontiguousarray(np.concatenate([sin, sin], axis=0)),
    )


def causal_data(c):
    groups = groups_for_core(c)
    biascols = np.zeros((128, 4 * 32), np.float32)
    masks = np.zeros((4, 8, 128, 512), np.float32)
    for j, g in enumerate(groups):
        qbase = 512 * g
        qmax = qbase + 511
        for kc in range(BOUND[j]):
            kpos = 128 * kc + np.arange(128)
            biascols[:, j * 32 + kc] = np.where(kpos > qmax, -1e30, 0.0)
        for ci, kc in enumerate(CAND[j]):
            kpos = 128 * kc + np.arange(128)
            qpos = qbase + np.arange(512)
            masks[j, ci] = np.where(kpos[:, None] > qpos[None, :], -1e30, 0.0)
    return biascols, masks.astype(BF)


def dropout_bits():
    import jax
    with jax.default_device(jax.devices("cpu")[0]):
        key = jax.random.key(42, impl="threefry2x32")
        u = np.asarray(jax.random.uniform(key, (B, S, D)))
    sel = (u >= 0.25).astype(np.uint8) + (u >= 0.5) + (u >= 0.75)
    return (sel & 1).astype(np.uint8), (sel >> 1).astype(np.uint8)



ENGINES = ("sync", "tensor", "vector", "scalar", "gpsimd")


class Prog:
    def __init__(self):
        self.nc = bass.Bass("TRN2", target_bir_lowering=False)
        self.ops = {e: [] for e in ENGINES}
        self.stack = contextlib.ExitStack()
        self.sems = {}
        self.cnt = {}
        self.bar_cnt = 0

    def sem(self, name):
        self.sems[name] = self.stack.enter_context(self.nc.semaphore(name))
        self.cnt[name] = 0
        return self.sems[name]

    def sbuf(self, name, shape, dt):
        return self.stack.enter_context(self.nc.sbuf_tensor(name, shape, dt))

    def psum(self, name, shape, dt=F32):
        return self.stack.enter_context(self.nc.psum_tensor(name, shape, dt))

    def emit(self, eng, fn):
        self.ops[eng].append(fn)

    def wait(self, eng, sem, val):
        h = self.sems[sem]
        self.emit(eng, lambda e, h=h, v=val: e.wait_ge(h, v))

    def wait_all(self, eng, sem):
        self.wait(eng, sem, self.cnt[sem])

    def op(self, eng, fn, sem=None, inc=1):
        if sem is None:
            self.emit(eng, fn)
            return None
        self.cnt[sem] += inc
        h = self.sems[sem]
        self.emit(eng, lambda e, fn=fn, h=h, n=inc: fn(e).then_inc(h, n))
        return self.cnt[sem]

    def dma(self, q, out, in_, sem, **kw):
        return self.op(
            q, lambda e, o=out, i=in_, kw=kw: e.dma_start(out=o, in_=i, **kw),
            sem=sem, inc=16,
        )

    def barrier(self, engines=ENGINES):
        if "bar" not in self.sems:
            self.sem("bar")
        if not hasattr(self, "bar_scr"):
            self.bar_scr = self.sbuf("bar_scr", [1, 2], F32)
        h = self.sems["bar"]
        scr = self.bar_scr
        self.bar_cnt += len(engines)
        tgt = self.bar_cnt
        for e in engines:
            if e in ("vector", "gpsimd"):
                self.emit(e, lambda eng, h=h: eng.engine_nop().then_inc(h, 1))
            elif e == "scalar":
                self.emit(e, lambda eng, h=h, a=scr[0:1, 0:1], b=scr[0:1, 1:2]:
                          eng.copy(b, a).then_inc(h, 1))
            else:
                self.emit(e, lambda eng, h=h: eng.sem_inc(h, 1))
        for e in engines:
            self.emit(e, lambda eng, h=h, v=tgt: eng.wait_ge(h, v))

    def finalize(self):
        blk = self.stack.enter_context(self.nc.Block())
        for e in ENGINES:
            getattr(blk, e)(lambda eng, ops=self.ops[e]: [f(eng) for f in ops])
        self.stack.close()
        return self.nc


def setup_common(P):
    for s in ("wld", "xld", "sld", "st", "pe", "act", "dve", "gp", "cc"):
        P.sem(s)
    P.sem("bar")
    c = {}
    c["ones_bf"] = P.sbuf("ones_bf", [128, 1], BF16)
    c["ones1_f"] = P.sbuf("ones1_f", [1, 128], F32)
    c["eps"] = P.sbuf("eps", [1, 1], F32)
    P.op("gpsimd", lambda e, t=c["ones_bf"]: e.memset(t[:, :], 1.0))
    P.op("gpsimd", lambda e, t=c["ones1_f"]: e.memset(t[:, :], 1.0))
    P.op("gpsimd", lambda e, t=c["eps"]: e.memset(t[:, :], EPS), sem="gp")
    P.barrier()
    return c


def emit_identity(P, ident):
    P.op("gpsimd", lambda e, t=ident: e.memset(t[:, :], 0.0))
    P.op(
        "gpsimd",
        lambda e, t=ident: e.affine_select(
            out=t[:, :], in_=t[:, :], compare_op=OP.not_equal,
            fill=1.0, base=0, pattern=[[-1, 128]], channel_multiplier=1,
        ),
        sem="gp",
    )
    P.barrier()
    return ident



def alloc_shared(P):
    Bf = {}
    Bf["xln"] = P.sbuf("xln", [128, NDC * R], BF16)
    Bf["wring"] = [P.sbuf(f"wring{i}", [128, 4096], BF16) for i in range(2)]
    Bf["scratch"] = P.sbuf("scratch", [128, R], F32)
    Bf["m_bc"] = P.sbuf("m_bc", [128, R], BF16)
    Bf["r_bc"] = P.sbuf("r_bc", [128, R], BF16)
    Bf["bstmp"] = P.sbuf("bstmp", [128, 2 * R], BF16)
    Bf["xres"] = [P.sbuf(f"xres{i}", [128, R], F32) for i in range(2)]
    Bf["m_row"] = Bf["scratch"][0:1, 0:R]
    Bf["r_row"] = Bf["xres"][0][0:1, 0:R]
    Bf["t0"] = Bf["m_bc"][0:1, 0:512]
    Bf["t1"] = Bf["r_bc"][0:1, 0:512]
    Bf["bias32"] = P.sbuf("bias32", [128, 32], F32)
    Bf["hT3"] = P.sbuf("hT3", [16, R], BF16)
    bt_all = [P.sbuf(f"bt_all{i}", [128, 256], BF16) for i in range(2)]
    Bf["bt"] = [
        [bt_all[i][0:16, 0:128], bt_all[i][32:48, 0:128],
         bt_all[i][64:80, 0:128], bt_all[i][0:16, 128:256]]
        for i in range(2)
    ]
    Bf["bits"] = [P.sbuf(f"bit{b}_s", [128, R], U8) for b in range(2)]
    Bf["ident_bf"] = P.sbuf("ident_bf", [128, 128], BF16)
    Bf["ident_f"] = P.sbuf("ident_f", [128, 128], F32)
    Bf["ps"] = [P.psum(f"ps{i}", [128, 512], F32) for i in range(8)]
    return Bf



def emit_stats_xln(P, xT, bufs, consts, nsq=2):
    xln = bufs["xln"]
    sq = bufs["bstmp"]
    scr = bufs["scratch"]
    m_bc, r_bc = bufs["m_bc"], bufs["r_bc"]
    ps = bufs["ps"]

    for dc in range(NDC):
        t_ld = P.dma(
            "gpsimd", xln[:, dc * R:(dc + 1) * R],
            xT[dc * 128:(dc + 1) * 128, :], "xld",
        )
        P.wait("scalar", "xld", t_ld)
        sqs_ = (dc % nsq) * R
        t_sq = P.op(
            "scalar",
            lambda e, i=xln[:, dc * R:(dc + 1) * R], o=sq[:, sqs_:sqs_ + R]:
                e.activation(o, i, AF.Square),
            sem="act",
        )
        P.wait("tensor", "act", t_sq)
        for tg in range(NTG):
            P.op(
                "tensor",
                lambda e, o=ps[tg][0:1, :], l=consts["ones_bf"][:, :],
                       r=xln[:, dc * R + tg * 512: dc * R + (tg + 1) * 512],
                       st=(dc == 0), sp=(dc == NDC - 1):
                    e.matmul(o, l, r, start=st, stop=sp),
            )
        t_pe = None
        for tg in range(NTG):
            t_pe = P.op(
                "tensor",
                lambda e, o=ps[4 + tg][0:1, :], l=consts["ones_bf"][:, :],
                       r=sq[:, sqs_ + tg * 512:sqs_ + (tg + 1) * 512],
                       st=(dc == 0), sp=(dc == NDC - 1):
                    e.matmul(o, l, r, start=st, stop=sp),
                sem="pe",
            )
        if dc + nsq < NDC:
            P.wait("scalar", "pe", t_pe)

    P.wait("vector", "pe", P.cnt["pe"])
    m_row = bufs["m_row"]
    r_row = bufs["r_row"]
    t0 = bufs["t0"]
    t1 = bufs["t1"]
    for tg in range(NTG):
        sl = slice(tg * 512, (tg + 1) * 512)
        P.op("vector", lambda e, o=m_row[:, sl], i=ps[tg][0:1, :]:
             e.tensor_scalar_mul(o, i, 1.0 / D))
        P.op("vector", lambda e, o=r_row[:, sl], i=ps[4 + tg][0:1, :]:
             e.tensor_scalar_mul(o, i, 1.0 / D))
        P.op("vector", lambda e, o=t0, a=m_row[:, sl]:
             e.tensor_mul(o, a, a))
        t_v = P.op("vector", lambda e, o=r_row[:, sl], a=r_row[:, sl], b=t0:
                   e.tensor_tensor(o, a, b, OP.subtract), sem="dve")
        P.wait("scalar", "dve", t_v)
        t_s = P.op("scalar", lambda e, o=t1, i=r_row[:, sl], b=consts["eps"][:, :]:
                   e.activation(o, i, AF.Sqrt, bias=b), sem="act")
        P.wait("vector", "act", t_s)
        P.op("vector", lambda e, o=r_row[:, sl], i=t1: e.reciprocal(o, i), sem="dve")

    P.wait("tensor", "dve", P.cnt["dve"])
    for i, (row, bc) in enumerate([(m_row, m_bc), (r_row, r_bc)]):
        for tg in range(NTG):
            sl = slice(tg * 512, (tg + 1) * 512)
            if i > 0 or tg > 0:
                P.wait("tensor", "dve", P.cnt["dve"])
            t_pe = P.op(
                "tensor",
                lambda e, o=ps[i % 2][:, :], l=consts["ones1_f"][:, :], r=row[:, sl]:
                    e.matmul(o, l, r, start=True, stop=True),
                sem="pe",
            )
            P.wait("vector", "pe", t_pe)
            P.op("vector", lambda e, o=bc[:, sl], i=ps[i % 2][:, :]:
                 e.tensor_copy(o, i), sem="dve")

    for dc in range(NDC):
        sl = slice(dc * R, (dc + 1) * R)
        P.op("vector", lambda e, o=sq[:, 0:R], a=xln[:, sl], b=m_bc[:, :]:
             e.tensor_tensor(o, a, b, OP.subtract))
        P.op("vector", lambda e, o=xln[:, sl], a=sq[:, 0:R], b=r_bc[:, :]:
             e.tensor_tensor(o, a, b, OP.mult), sem="dve")
    P.barrier()



def emit_ws_loop(P, oc_count, w_src, bufs, epilogue, rhs_buf=None,
                 ping_pong=True, extra_pe_wait=None):
    xln = rhs_buf if rhs_buf is not None else bufs["xln"]
    ps = bufs["ps"]
    groups = [ps[0:4], ps[4:8]] if ping_pong else [ps[0:4], ps[0:4]]
    free_tgt = [0, 0]
    wslot_free = [0, 0]
    for oc in range(oc_count):
        par = oc % 2
        if wslot_free[par]:
            P.wait("gpsimd", "pe", wslot_free[par])
        t_w = P.dma("gpsimd", bufs["wring"][par][:, :], w_src(oc), "wld")
        P.wait("tensor", "wld", t_w)
        ft = free_tgt[par] if ping_pong else max(free_tgt)
        if ft:
            P.wait("tensor", "dve", ft)
        if extra_pe_wait is not None:
            extra_pe_wait(oc)
        grp = groups[par]
        t_pe = None
        for dc in range(NDC):
            for tg in range(NTG):
                t_pe = P.op(
                    "tensor",
                    lambda e, o=grp[tg][:, :],
                           l=bufs["wring"][par][:, dc * 128:(dc + 1) * 128],
                           r=xln[:, dc * R + tg * 512: dc * R + (tg + 1) * 512],
                           st=(dc == 0), sp=(dc == NDC - 1):
                        e.matmul(o, l, r, start=st, stop=sp),
                    sem=("pe" if (dc == NDC - 1 and tg == NTG - 1) else None),
                )
        wslot_free[par] = P.cnt["pe"]
        free_tgt[par] = epilogue(oc, grp)



def emit_P1(P, consts, Bf, io):
    w1, w2T, maskd, out = io["w1_t"], io["w2T_r"], io["mask_s"], io["attn_t"]
    w2sb = Bf["xln"][:, 0:32 * 512]
    wring = Bf["wring"]
    scr = Bf["scratch"]
    msk = [scr[:, 0:512], scr[:, 512:1024]]
    och = [Bf["bstmp"][:, 0:512], Bf["bstmp"][:, 512:1024]]
    ps = Bf["ps"]

    t_w2 = P.dma("gpsimd", w2sb, w2T[:, :], "xld")
    P.wait("tensor", "xld", t_w2)

    wfree = [0, 0]
    ofree = [0, 0]
    st_free = [0, 0]
    for sc in range(32):
        par = sc % 2
        if wfree[par]:
            P.wait("gpsimd", "pe", wfree[par])
        t_w = P.dma("gpsimd", wring[par][:, :], w1[sc], "wld")
        t_m = P.dma("sync", msk[par], maskd[sc * 128:(sc + 1) * 128, :], "sld")
        P.wait("tensor", "wld", t_w)
        if ofree[par]:
            P.wait("tensor", "dve", ofree[par])
        t_pe = None
        for dc in range(NDC):
            t_pe = P.op(
                "tensor",
                lambda e, o=ps[par][:, :], l=wring[par][:, dc * 128:(dc + 1) * 128],
                       r=w2sb[:, dc * 512:(dc + 1) * 512],
                       st=(dc == 0), sp=(dc == NDC - 1):
                    e.matmul(o, l, r, start=st, stop=sp),
                sem=("pe" if dc == NDC - 1 else None),
            )
        wfree[par] = t_pe
        P.wait("vector", "pe", t_pe)
        P.wait("vector", "sld", t_m)
        if st_free[par]:
            P.wait_all("vector", "st")
        t_v = P.op("vector", lambda e, o=och[par], a=ps[par][:, :], b=msk[par]:
                   e.tensor_tensor(o, a, b, OP.mult), sem="dve")
        ofree[par] = t_v
        P.wait("sync", "dve", t_v)
        for tcl in range(4):
            st_free[par] = P.dma("sync", out[tcl, :, sc, :],
                                 och[par][:, tcl * 128:(tcl + 1) * 128], "st")
    P.wait("sync", "st", P.cnt["st"])
    P.barrier()



def emit_P2(P, consts, Bf, io):
    xT, attn, b2d, xo = io["xT"], io["attn_all"], io["b2"], io["xT1"]
    b2c = Bf["bias32"]
    P.dma("gpsimd", b2c[:, :], b2d.rearrange("(c p) -> p c", p=128), "xld",
          allow_slow_non_contiguous=True)
    emit_stats_xln(P, xT, Bf, consts)

    gel = Bf["bstmp"]
    xres = Bf["xres"]
    state = {"out_tgt": [0, 0], "st_x": [0, 0]}

    def epilogue(oc, grp):
        par = oc % 2
        if state["out_tgt"][par]:
            P.wait("sync", "dve", state["out_tgt"][par])
        if state["st_x"][par]:
            P.wait_all("sync", "st")
        t_x = P.dma("sync", xres[par][:, :], xT[oc * 128:(oc + 1) * 128, :], "sld")
        t_pe = P.cnt["pe"]
        P.wait("scalar", "pe", t_pe)
        if state["out_tgt"][par]:
            P.wait("scalar", "dve", state["out_tgt"][par])
        t_act = None
        for tg in range(NTG):
            t_act = P.op(
                "scalar",
                lambda e, o=gel[:, (par * R) + tg * 512:(par * R) + (tg + 1) * 512],
                       i=grp[tg][:, :], b=b2c[:, oc:oc + 1]:
                    e.activation(o, i, AF.Gelu, bias=b),
                sem="act",
            )
        P.wait("vector", "act", t_act)
        P.wait("vector", "sld", t_x)
        t_v = P.op(
            "vector",
            lambda e, o=xres[par][:, :], a=gel[:, par * R:(par + 1) * R], b=xres[par][:, :]:
                e.tensor_tensor(o, a, b, OP.add),
            sem="dve",
        )
        state["out_tgt"][par] = t_v
        P.wait("sync", "dve", t_v)
        state["st_x"][par] = P.dma(
            "sync", xo[oc * 128:(oc + 1) * 128, :], xres[par][:, :], "st")
        return t_v

    emit_ws_loop(P, 32, lambda oc: attn[oc], Bf, epilogue)
    P.wait("sync", "st", P.cnt["st"])
    P.barrier()



def emit_P3(P, consts, Bf, io):
    xT, WT, gwT = io["xT1"], io["WT_t"], io["gwT_r"]
    icsd, ocsd, biasd, xo = io["ics"], io["ocs"], io["ffn_bias"], io["xT2"]
    bias_c = Bf["bias32"]
    scr = Bf["scratch"]
    xln = Bf["xln"]
    ps = Bf["ps"]
    ident = Bf["ident_bf"]
    routT = Bf["m_bc"][0:8, 0:R]
    gw_sb = Bf["r_bc"][:, 0:256]
    ics_s = [Bf["r_bc"][0:8, 256 + i * 128:256 + (i + 1) * 128] for i in range(2)]
    ocs_s = [Bf["r_bc"][0:8, 512 + i * 128:512 + (i + 1) * 128] for i in range(2)]
    rout_b = Bf["r_bc"][:, 768:776]
    rout = scr[:, 0:8]
    rsml = scr[:, 8:12]

    P.dma("gpsimd", bias_c[:, :], biasd.rearrange("(c p) -> p c", p=128), "xld",
          allow_slow_non_contiguous=True)
    emit_stats_xln(P, xT, Bf, consts)

    t_gw = P.dma("gpsimd", gw_sb, gwT[:, :], "wld")
    P.wait("tensor", "wld", t_gw)
    lg_free = [0, 0]
    for tt in range(NTT):
        par = tt % 2
        if lg_free[par]:
            P.wait("tensor", "act", lg_free[par])
        t_pe = None
        for dc in range(NDC):
            t_pe = P.op(
                "tensor",
                lambda e, o=ps[par][:, 0:8],
                       l=xln[:, dc * R + tt * 128: dc * R + (tt + 1) * 128],
                       r=gw_sb[:, dc * 8:(dc + 1) * 8],
                       st=(dc == 0), sp=(dc == NDC - 1):
                    e.matmul(o, l, r, start=st, stop=sp),
                sem=("pe" if dc == NDC - 1 else None),
            )
        P.wait("vector", "pe", t_pe)
        t_mx = P.op("vector", lambda e, o=rsml[:, 1:2], i=ps[par][:, 0:8]:
                    e.reduce_max(o, i, axis=AX, negate=True), sem="dve")
        P.wait("scalar", "dve", t_mx)
        t_a = P.op("scalar", lambda e, o=rout, i=ps[par][:, 0:8], b=rsml[:, 1:2],
                   a_=rsml[:, 2:3]:
                   e.activation(o, i, AF.Exp, bias=b, accum_out=a_), sem="act")
        lg_free[par] = t_a
        P.wait("vector", "act", t_a)
        t_rc = P.op("vector", lambda e, o=rsml[:, 2:3], i=rsml[:, 2:3]:
                    e.reciprocal(o, i), sem="dve")
        P.wait("scalar", "dve", t_rc)
        t_cb = P.op("scalar", lambda e, o=rsml[:, 3:4], i=rsml[:, 2:3]:
                    e.copy(o, i), sem="act")
        P.wait("vector", "act", t_cb)
        t_cp = P.op("vector", lambda e, o=rout_b, i=rout, s=rsml[:, 3:4]:
                    e.tensor_scalar(o, i, s, None, OP.mult), sem="dve")
        P.wait("tensor", "dve", t_cp)
        t_t = P.op("tensor", lambda e, o=ps[2][:, :].bitcast(BF16)[0:8, 0:128],
                   i=rout_b, d=ident[:, :]:
                   e.transpose(o, i, d), sem="pe")
        P.wait("vector", "pe", t_t)
        P.op("vector", lambda e, o=routT[:, tt * 128:(tt + 1) * 128],
             i=ps[2][:, :].bitcast(BF16)[0:8, 0:128]:
             e.tensor_copy(o, i), sem="dve")
    P.barrier()

    for dc in range(NDC):
        par_i = dc % 2
        if dc >= 2:
            P.wait("gpsimd", "pe", P.cnt["pe"] - 1)
        t_i = P.dma("gpsimd", ics_s[par_i], icsd[:, dc * 128:(dc + 1) * 128], "wld")
        P.wait("tensor", "wld", t_i)
        t_pe = None
        for tg in range(NTG):
            t_pe = P.op(
                "tensor",
                lambda e, o=ps[4 + tg][:, :], l=ics_s[par_i],
                       r=routT[:, tg * 512:(tg + 1) * 512]:
                    e.matmul(o, l, r, start=True, stop=True),
                sem=("pe" if tg == NTG - 1 else None),
            )
        P.wait("vector", "pe", t_pe)
        t_v = None
        for tg in range(NTG):
            sl = slice(dc * R + tg * 512, dc * R + (tg + 1) * 512)
            t_v = P.op(
                "vector",
                lambda e, o=xln[:, sl], a=xln[:, sl], b=ps[4 + tg][:, :]:
                    e.tensor_tensor(o, a, b, OP.mult),
                sem="dve",
            )
        P.wait("tensor", "dve", t_v)
    P.barrier()

    xres = Bf["xres"]
    state = {"out_tgt": [0, 0], "st_x": [0, 0], "os_pe": [0, 0]}

    def epilogue(oc, grp):
        par = oc % 2
        if state["out_tgt"][par]:
            P.wait("sync", "dve", state["out_tgt"][par])
        if state["st_x"][par]:
            P.wait_all("sync", "st")
        t_x = P.dma("sync", xres[par][:, :], xT[oc * 128:(oc + 1) * 128, :], "sld")
        if state["os_pe"][par]:
            P.wait("gpsimd", "pe", state["os_pe"][par])
        t_oc = P.dma("gpsimd", ocs_s[par], ocsd[:, oc * 128:(oc + 1) * 128], "wld")
        P.wait("tensor", "wld", t_oc)
        t_v = None
        for tg in range(NTG):
            if tg > 0:
                P.wait("tensor", "dve", P.cnt["dve"])
            t_os = P.op(
                "tensor",
                lambda e, o=ps[4][:, :], l=ocs_s[par],
                       r=routT[:, tg * 512:(tg + 1) * 512]:
                    e.matmul(o, l, r, start=True, stop=True),
                sem="pe",
            )
            P.wait("scalar", "pe", t_os)
            t_osc = P.op("scalar", lambda e, o=scr[:, 0:512], i=ps[4][:, :]:
                         e.copy(o, i), sem="act")
            P.wait("vector", "act", t_osc)
            P.op("vector", lambda e, o=scr[:, 512:1024], a=grp[tg][:, :], b=scr[:, 0:512]:
                 e.tensor_tensor(o, a, b, OP.mult))
            if tg == 0:
                P.wait("vector", "sld", t_x)
            t_v = P.op(
                "vector",
                lambda e, o=xres[par][:, tg * 512:(tg + 1) * 512], a=scr[:, 512:1024],
                       s=bias_c[:, oc:oc + 1], b=xres[par][:, tg * 512:(tg + 1) * 512]:
                    e.scalar_tensor_tensor(o, a, s, b, OP.add, OP.add),
                sem="dve",
            )
        state["out_tgt"][par] = t_v
        state["os_pe"][par] = P.cnt["pe"]
        P.wait("sync", "dve", t_v)
        state["st_x"][par] = P.dma(
            "sync", xo[oc * 128:(oc + 1) * 128, :], xres[par][:, :], "st")
        return t_v

    def extra_sign(oc):
        par = oc % 2
        t_w = P.cnt["wld"]
        P.wait("scalar", "wld", t_w)
        t_s = P.op("scalar", lambda e, t=Bf["wring"][par][:, :]:
                   e.sign(t[:, :], t[:, :]), sem="act")
        P.wait("tensor", "act", t_s)

    emit_ws_loop(P, 32, lambda oc: WT[oc], Bf, epilogue, ping_pong=False,
                 extra_pe_wait=extra_sign)
    P.wait("sync", "st", P.cnt["st"])
    P.barrier()



def emit_P4(P, consts, Bf, io):
    xT, wq = io["xT2"], io["wqkv_t"]
    cosd, sind = io["cos2"], io["sin2"]
    QT, KT, VT = io["QT"], io["KT"], io["Vt"]
    ident = Bf["ident_bf"]
    emit_stats_xln(P, xT, Bf, consts)

    xln = Bf["xln"]
    ps = Bf["ps"]
    scr = Bf["scratch"]
    bstmp = Bf["bstmp"]
    wring = Bf["wring"]
    cos2 = Bf["m_bc"][:, :]
    sin2 = Bf["r_bc"][:, :]
    t_cs = P.dma("gpsimd", cos2, cosd[:, :], "xld")
    t_cs = P.dma("gpsimd", sin2, sind[:, :], "xld")
    P.wait("vector", "xld", t_cs)

    state = {"free": 0, "wfree": [0, 0], "st_qk": 0}
    for pair in range(20):
        oc0, oc1 = 2 * pair, 2 * pair + 1
        for k, oc in enumerate((oc0, oc1)):
            par = k
            if state["wfree"][par]:
                P.wait("gpsimd", "pe", state["wfree"][par])
            t_w = P.dma("gpsimd", wring[par][:, :], wq[oc], "wld")
            P.wait("tensor", "wld", t_w)
        if state["free"]:
            P.wait("tensor", "dve", state["free"])
        t_pe = None
        for k in range(2):
            for dc in range(NDC):
                for tg in range(NTG):
                    t_pe = P.op(
                        "tensor",
                        lambda e, o=ps[4 * k + tg][:, :],
                               l=wring[k][:, dc * 128:(dc + 1) * 128],
                               r=xln[:, dc * R + tg * 512: dc * R + (tg + 1) * 512],
                               st=(dc == 0), sp=(dc == NDC - 1):
                            e.matmul(o, l, r, start=st, stop=sp),
                        sem=("pe" if (dc == NDC - 1 and tg == NTG - 1) else None),
                    )
        state["wfree"] = [t_pe, t_pe]
        P.wait("vector", "pe", t_pe)
        if state["st_qk"]:
            P.wait_all("vector", "st")
        x1 = [ps[tg][:, :] for tg in range(4)]
        x2 = [ps[4 + tg][:, :] for tg in range(4)]
        t_v = None
        for tg in range(NTG):
            sl = slice(tg * 512, (tg + 1) * 512)
            P.op("vector", lambda e, o=scr[:, 0:512], a=x1[tg], b=cos2[:, sl]:
                 e.tensor_tensor(o, a, b, OP.mult))
            P.op("vector", lambda e, o=scr[:, 512:1024], a=x2[tg], b=sin2[:, sl]:
                 e.tensor_tensor(o, a, b, OP.mult))
            P.op("vector", lambda e, o=bstmp[:, sl], a=scr[:, 0:512], b=scr[:, 512:1024]:
                 e.tensor_tensor(o, a, b, OP.subtract))
            P.op("vector", lambda e, o=scr[:, 0:512], a=x1[tg], b=sin2[:, sl]:
                 e.tensor_tensor(o, a, b, OP.mult))
            P.op("vector", lambda e, o=scr[:, 512:1024], a=x2[tg], b=cos2[:, sl]:
                 e.tensor_tensor(o, a, b, OP.mult))
            t_v = P.op("vector", lambda e, o=bstmp[:, R + tg * 512:R + (tg + 1) * 512],
                       a=scr[:, 0:512], b=scr[:, 512:1024]:
                       e.tensor_tensor(o, a, b, OP.add), sem="dve")
        state["free"] = t_v
        P.wait("sync", "dve", t_v)
        if pair < 16:
            P.dma("sync", QT[oc0], bstmp[:, 0:R], "st")
            state["st_qk"] = P.dma("sync", QT[oc1], bstmp[:, R:2 * R], "st")
        else:
            P.dma("sync", KT[oc0 - 32], bstmp[:, 0:R], "st")
            state["st_qk"] = P.dma("sync", KT[oc1 - 32], bstmp[:, R:2 * R], "st")
    P.barrier()

    vfree = [0, 0]
    for ocv in range(8):
        oc = 40 + ocv
        par = ocv % 2
        if vfree[par]:
            P.wait("gpsimd", "pe", vfree[par])
        t_w = P.dma("gpsimd", wring[par][:, :], wq[oc], "wld")
        P.wait("tensor", "wld", t_w)
        t_pe = None
        for dc in range(NDC):
            for tg in range(NTG):
                t_pe = P.op(
                    "tensor",
                    lambda e, o=ps[tg][:, :],
                           l=wring[par][:, dc * 128:(dc + 1) * 128],
                           r=xln[:, dc * R + tg * 512: dc * R + (tg + 1) * 512],
                           st=(dc == 0), sp=(dc == NDC - 1):
                        e.matmul(o, l, r, start=st, stop=sp),
                    sem=("pe" if (dc == NDC - 1 and tg == NTG - 1) else None),
                )
        vfree[par] = t_pe
        P.wait("vector", "pe", t_pe)
        P.wait("vector", "st", P.cnt["st"])
        t_v = None
        for tg in range(NTG):
            t_v = P.op("vector", lambda e, o=bstmp[:, tg * 512:(tg + 1) * 512],
                       i=ps[tg][:, :]: e.tensor_copy(o, i), sem="dve")
        P.wait("tensor", "dve", t_v)
        for q4 in range(4):
            tpar = 4 + q4 % 2
            if q4 >= 2:
                P.wait("tensor", "dve", P.cnt["dve"])
            t_t = None
            for i in range(4):
                tt = q4 * 4 + i
                t_t = P.op(
                    "tensor",
                    lambda e, o=ps[tpar][:, :].bitcast(BF16)[:, i * 128:(i + 1) * 128],
                           i_=bstmp[:, tt * 128:(tt + 1) * 128], d=ident[:, :]:
                        e.transpose(o, i_, d),
                    sem=("pe" if i == 3 else None),
                )
            P.wait("vector", "pe", t_t)
            t_c = P.op("vector", lambda e, o=bstmp[:, R + q4 * 512:R + (q4 + 1) * 512],
                       i=ps[tpar][:, :].bitcast(BF16)[:, 0:512]:
                       e.tensor_copy(o, i), sem="dve")
            P.wait("sync", "dve", t_c)
            for i in range(4):
                tt = q4 * 4 + i
                P.dma("sync", VT[ocv, tt],
                      bstmp[:, R + q4 * 512 + i * 128: R + q4 * 512 + (i + 1) * 128],
                      "st")
    P.wait("sync", "st", P.cnt["st"])
    P.barrier()



def emit_P5(P, consts, Bf, io):
    QT, KTp, VTp = io["QT"], io["KT_pair"], io["VT_pair"]
    bcd, mkd, AOT = io["biascols"], io["cmasks"], io["AOT"]
    xln = Bf["xln"]
    scr = Bf["scratch"]
    ksb = xln[:, 0:4096]
    vsb = xln[:, 4096:8192]
    qsb = xln[:, 8192:10240]
    sqs = xln[:, 10240:14336]
    mks = xln[:, 14336:14336 + 4 * 8 * 512]
    pt = [xln[:, 30720:31232], xln[:, 31232:31744], xln[:, 32768:33280]]
    aosb = [xln[:, 31744:32256], xln[:, 32256:32768]]
    ptacc = xln[:, 33280:33792]
    bcs = Bf["bias32"]
    bcs = scr[:, 0:128]
    bcr = scr[:, 128:256]
    rbc = scr[:, 256:768]
    srow = scr[0:1, 768:1280]
    nrm = scr[0:1, 1280:1296]
    nrm2 = scr[0:1, 1296:1312]
    nbc = scr[:, 1312:1313]
    ps = Bf["ps"]
    ps_s = [ps[0], ps[1], ps[5]]
    ps_pv = [ps[2], ps[3]]
    ps_sum = ps[4]
    ps_n = ps[7]
    ps_rbc = ps[6]
    ps_sm = ps[7]

    t_b = P.dma("gpsimd", bcs, bcd[:, :], "xld")
    for j in range(4):
        for ci in range(8):
            P.dma("gpsimd", mks[:, (j * 8 + ci) * 512:(j * 8 + ci + 1) * 512],
                  mkd[j, ci], "xld")
    P.wait_all("gpsimd", "xld")
    P.barrier()

    gmap = {}
    for par, groups in ((0, GROUPS_EVEN), (1, GROUPS_ODD)):
        for slot, g in enumerate(groups):
            gmap[g] = (par, slot)

    def load_kt(hk):
        p2, e = hk // 2, hk % 2
        tgt = None
        for g in range(8):
            rank, slot = gmap[g]
            for half, oc in ((0, 2 * p2), (1, 2 * p2 + 1)):
                tgt = P.dma(
                    "sync", ksb[64 * half:64 * half + 64, 512 * g:512 * (g + 1)],
                    KTp[rank, oc, 64 * e:64 * e + 64, 512 * slot:512 * (slot + 1)],
                    "sld",
                )
        return tgt

    def load_v(hk):
        tgt = None
        for kc in range(32):
            g = kc // 4
            rank, slot = gmap[g]
            tgt = P.dma("sync", vsb[:, kc * 128:(kc + 1) * 128],
                        VTp[rank, hk, slot * 4 + kc % 4], "sld")
        return tgt

    def load_q(h):
        p2, e = h // 2, h % 2
        P.dma("sync", qsb[0:64, :], QT[2 * p2, 64 * e:64 * e + 64, :], "sld")
        return P.dma("sync", qsb[64:128, :], QT[2 * p2 + 1, 64 * e:64 * e + 64, :], "sld")

    def norms2(src, width, out_slot):
        t_a = P.op("scalar", lambda e, o=sqs[:, 0:width], i=src[:, 0:width]:
                   e.activation(o, i, AF.Square), sem="act")
        P.wait("tensor", "act", t_a)
        t_v = None
        for tg in range(width // 512):
            t_pe = P.op(
                "tensor",
                lambda e, o=ps_n[0:1, :], l=consts["ones_bf"][:, :],
                       r=sqs[:, tg * 512:(tg + 1) * 512]:
                    e.matmul(o, l, r, start=True, stop=True),
                sem="pe",
            )
            P.wait("vector", "pe", t_pe)
            t_v = P.op("vector", lambda e, o=nrm[0:1, out_slot + tg:out_slot + tg + 1],
                       i=ps_n[0:1, :]: e.reduce_max(o, i, axis=AX), sem="dve")
            P.wait("tensor", "dve", t_v)
        return t_v

    state = {"aot_free": [0, 0], "h_pe": 0, "srow_free": 0, "rbc_free": 0,
             "prev_done": 0, "prev_acc": 0}
    for hk in range(8):
        if state["h_pe"]:
            P.wait("sync", "pe", state["h_pe"])
        t_k = load_kt(hk)
        t_vv = load_v(hk)
        P.wait_all("scalar", "sld")
        t_nk = norms2(ksb, S, 8)
        P.wait("scalar", "dve", t_nk)
        t_bk = P.op("scalar", lambda e, o=nrm2[0:1, 8:16], i=nrm[0:1, 8:16]:
                    e.copy(o, i), sem="act")
        P.wait("vector", "act", t_bk)
        P.op("vector", lambda e, o=nrm[0:1, 0:1], i=nrm2[0:1, 8:16]:
             e.reduce_max(o, i, axis=AX), sem="dve")
        P.wait_all("tensor", "sld")
        for hq4 in range(4):
            h = 4 * hk + hq4
            if state["h_pe"]:
                P.wait("sync", "pe", state["h_pe"])
            t_q = load_q(h)
            P.wait_all("scalar", "sld")
            t_nq = norms2(qsb, R, 4)
            P.wait("scalar", "dve", t_nq)
            t_bq = P.op("scalar", lambda e, o=nrm2[0:1, 4:8], i=nrm[0:1, 4:8]:
                        e.copy(o, i), sem="act")
            P.wait("vector", "act", t_bq)
            P.op("vector", lambda e, o=nrm[0:1, 1:2], i=nrm2[0:1, 4:8]:
                 e.reduce_max(o, i, axis=AX), sem="dve")
            P.wait("scalar", "dve", P.cnt["dve"])
            t_bb = P.op("scalar", lambda e, o=nrm2[0:1, 0:2], i=nrm[0:1, 0:2]:
                        e.copy(o, i), sem="act")
            P.wait("vector", "act", t_bb)
            P.op("vector", lambda e, o=nrm[0:1, 2:3], a=nrm2[0:1, 0:1], b=nrm2[0:1, 1:2]:
                 e.tensor_tensor(o, a, b, OP.mult), sem="dve")
            P.wait("scalar", "dve", P.cnt["dve"])
            P.op("scalar", lambda e, o=nrm[0:1, 2:3], i=nrm[0:1, 2:3]:
                 e.activation(o, i, AF.Sqrt), sem="act")
            P.wait("vector", "act", P.cnt["act"])
            t_b2 = P.op("vector", lambda e, o=nrm[0:1, 3:4], i=nrm[0:1, 2:3]:
                        e.tensor_scalar_mul(o, i, -ATT_SCALE), sem="dve")
            P.wait("tensor", "dve", t_b2)
            t_pe = P.op("tensor", lambda e, o=ps_sm[:, 0:1], l=consts["ones1_f"][:, :],
                        r=nrm[0:1, 3:4]: e.matmul(o, l, r, start=True, stop=True),
                        sem="pe")
            P.wait("scalar", "pe", t_pe)
            t_nbc = P.op("scalar", lambda e, o=nbc, i=ps_sm[:, 0:1]:
                         e.copy(o, i), sem="act")
            P.wait("vector", "act", t_nbc)
            t_bc = P.op("vector", lambda e, o=bcr, i=bcs, s=nbc:
                        e.tensor_scalar(o, i, s, None, OP.add), sem="dve")
            P.wait("scalar", "dve", t_bc)
            for j in range(4):
                jpar = j % 2
                if state["aot_free"][jpar]:
                    P.wait("tensor", "dve", state["aot_free"][jpar])
                nkc = BOUND[j]

                pv_cnt = {}
                acc_cnt = {}

                def emit_S(kc, j=j, pv_cnt=pv_cnt, acc_cnt=acc_cnt):
                    spar = kc % 3
                    t_s = P.op(
                        "tensor",
                        lambda e, o=ps_s[spar][:, :], l=ksb[:, kc * 128:(kc + 1) * 128],
                               r=qsb[:, j * 512:(j + 1) * 512]:
                            e.matmul(o, l, r, start=True, stop=True),
                        sem="pe",
                    )
                    if kc in CAND[j]:
                        ci = kc - CAND[j][0]
                        P.wait("vector", "pe", t_s)
                        t_m = P.op(
                            "vector",
                            lambda e, o=ps_s[spar][:, :], a=ps_s[spar][:, :],
                                   b=mks[:, (j * 8 + ci) * 512:(j * 8 + ci + 1) * 512]:
                                e.tensor_tensor(o, a, b, OP.add),
                            sem="dve",
                        )
                        P.wait("scalar", "dve", t_m)
                    else:
                        P.wait("scalar", "pe", t_s)
                    g = pv_cnt.get(kc - 3, state["prev_done"])
                    if g:
                        P.wait("scalar", "pe", g)
                    ga = acc_cnt.get(kc - 3, state["prev_acc"])
                    if ga:
                        P.wait("scalar", "dve", ga)
                    t_e = P.op(
                        "scalar",
                        lambda e, o=pt[spar], i=ps_s[spar][:, :],
                               b=bcr[:, j * 32 + kc:j * 32 + kc + 1]:
                            e.activation(o, i, AF.Exp, bias=b, scale=ATT_SCALE),
                        sem="act",
                    )
                    return t_e

                t_e = {}
                for kk in range(min(2, nkc)):
                    t_e[kk] = emit_S(kk)
                t_q4 = 0
                for kc in range(nkc):
                    spar = kc % 3
                    if kc + 2 < nkc:
                        t_e[kc + 2] = emit_S(kc + 2)
                    t_ek = t_e.pop(kc)
                    P.wait("tensor", "act", t_ek)
                    P.wait("vector", "act", t_ek)
                    if kc % 4 == 0:
                        if t_q4:
                            P.wait("vector", "pe", t_q4)
                        t_pa = P.op("vector", lambda e, o=ptacc, i=pt[spar]:
                                    e.tensor_copy(o, i), sem="dve")
                    else:
                        t_pa = P.op("vector", lambda e, o=ptacc, a=ptacc, b=pt[spar]:
                                    e.tensor_tensor(o, a, b, OP.add), sem="dve")
                    acc_cnt[kc] = t_pa
                    if kc % 4 == 3 or kc == nkc - 1:
                        P.wait("tensor", "dve", t_pa)
                        t_q4 = P.op(
                            "tensor",
                            lambda e, o=ps_sum[0:1, :], l=consts["ones_bf"][:, :],
                                   r=ptacc, st=(kc < 4), sp=(kc == nkc - 1):
                                e.matmul(o, l, r, start=st, stop=sp),
                            sem="pe",
                        )
                    pv_cnt[kc] = P.op(
                        "tensor",
                        lambda e, o=ps_pv[jpar][:, :], l=vsb[:, kc * 128:(kc + 1) * 128],
                               r=pt[spar], st=(kc == 0), sp=(kc == nkc - 1):
                            e.matmul(o, l, r, start=st, stop=sp),
                        sem="pe",
                    )
                t_done = P.cnt["pe"]
                state["prev_done"] = t_done
                state["prev_acc"] = acc_cnt[nkc - 1]
                P.wait("scalar", "pe", t_done)
                if state["srow_free"]:
                    P.wait("scalar", "dve", state["srow_free"])
                t_cs = P.op("scalar", lambda e, o=srow, i=ps_sum[0:1, :]:
                            e.copy(o, i), sem="act")
                P.wait("vector", "act", t_cs)
                t_rc = P.op("vector", lambda e, o=srow, i=srow:
                            e.reciprocal(o, i), sem="dve")
                state["srow_free"] = t_rc
                P.wait("tensor", "dve", t_rc)
                t_pe2 = P.op("tensor", lambda e, o=ps_rbc[:, :], l=consts["ones1_f"][:, :],
                             r=srow: e.matmul(o, l, r, start=True, stop=True),
                             sem="pe")
                P.wait("scalar", "pe", t_pe2)
                if state["rbc_free"]:
                    P.wait("scalar", "dve", state["rbc_free"])
                t_rbc = P.op("scalar", lambda e, o=rbc, i=ps_rbc[:, :]:
                             e.copy(o, i), sem="act")
                P.wait("vector", "act", t_rbc)
                if state.setdefault("ao_st", [0, 0])[jpar]:
                    P.wait_all("vector", "st")
                t_ao = P.op("vector", lambda e, o=aosb[jpar], a=ps_pv[jpar][:, :], b=rbc:
                            e.tensor_tensor(o, a, b, OP.mult), sem="dve")
                state["rbc_free"] = t_ao
                state["aot_free"][jpar] = t_ao
                P.wait("sync", "dve", t_ao)
                state["ao_st"][jpar] = P.dma(
                    "sync", AOT[h, :, j * 512:(j + 1) * 512], aosb[jpar], "st")
            state["h_pe"] = P.cnt["pe"]
    P.wait("sync", "st", P.cnt["st"])
    P.barrier()



def emit_P6(P, consts, Bf, io):
    xT, AOTd, wo, xo = io["xT2"], io["AOT"], io["wo_t"], io["xT3"]
    aot = Bf["xln"]
    t_l = None
    for hc in range(32):
        t_l = P.dma("gpsimd", aot[:, hc * R:(hc + 1) * R], AOTd[hc], "xld")
    P.wait("tensor", "xld", t_l)
    P.barrier()

    xres = Bf["xres"]
    state = {"out_tgt": [0, 0], "st_x": [0, 0]}

    def epilogue(oc, grp):
        par = oc % 2
        if state["out_tgt"][par]:
            P.wait("sync", "dve", state["out_tgt"][par])
        if state["st_x"][par]:
            P.wait_all("sync", "st")
        t_x = P.dma("sync", xres[par][:, :], xT[oc * 128:(oc + 1) * 128, :], "sld")
        t_pe = P.cnt["pe"]
        P.wait("vector", "pe", t_pe)
        P.wait("vector", "sld", t_x)
        t_v = None
        for tg in range(NTG):
            t_v = P.op(
                "vector",
                lambda e, o=xres[par][:, tg * 512:(tg + 1) * 512], a=grp[tg][:, :],
                       b=xres[par][:, tg * 512:(tg + 1) * 512]:
                    e.tensor_tensor(o, a, b, OP.add),
                sem="dve",
            )
        state["out_tgt"][par] = t_v
        P.wait("sync", "dve", t_v)
        state["st_x"][par] = P.dma(
            "sync", xo[oc * 128:(oc + 1) * 128, :], xres[par][:, :], "st")
        return t_v

    emit_ws_loop(P, 32, lambda oc: wo[oc], Bf, epilogue)
    P.wait("sync", "st", P.cnt["st"])
    P.barrier()



def emit_P7(P, consts, Bf, io):
    xT, W0T, ATd = io["xT3"], io["W0T_t"], io["AT_r"]
    BTd, b0d, bit0d, bit1d, xo = io["BT"], io["b0"], io["bit0"], io["bit1"], io["xT4"]
    scr = Bf["scratch"]
    bstmp = Bf["bstmp"]
    xln = Bf["xln"]
    ps = Bf["ps"]
    ident = Bf["ident_bf"]
    b0c = Bf["bias32"]
    bt = Bf["bt"]
    bits = Bf["bits"]
    at_sb = Bf["r_bc"][:, :]
    hT = [Bf["m_bc"][0:16, :], Bf["m_bc"][32:48, :], Bf["m_bc"][64:80, :],
          Bf["hT3"][:, :]]
    ysb = bstmp[:, R:2 * R]

    P.dma("gpsimd", b0c[:, :], b0d.rearrange("(c p) -> p c", p=128), "xld",
          allow_slow_non_contiguous=True)
    emit_stats_xln(P, xT, Bf, consts, nsq=1)
    t_at = P.dma("gpsimd", at_sb, ATd[:, :], "xld")

    P.wait("tensor", "xld", t_at)
    for tt in range(NTT):
        par = tt % 2
        t_pe = None
        for dc in range(NDC):
            t_pe = P.op(
                "tensor",
                lambda e, o=ps[par][:, 0:64],
                       l=xln[:, dc * R + tt * 128: dc * R + (tt + 1) * 128],
                       r=at_sb[:, dc * 64:(dc + 1) * 64],
                       st=(dc == 0), sp=(dc == NDC - 1):
                    e.matmul(o, l, r, start=st, stop=sp),
                sem=("pe" if dc == NDC - 1 else None),
            )
        P.wait("vector", "pe", t_pe)
        t_c = P.op("vector", lambda e, o=bstmp[:, 0:64], i=ps[par][:, 0:64]:
                   e.tensor_copy(o, i), sem="dve")
        P.wait("tensor", "dve", t_c)
        for l in range(4):
            t_t = P.op(
                "tensor",
                lambda e, o=ps[2][:, :].bitcast(BF16)[0:16, 0:128],
                       i=bstmp[:, l * 16:(l + 1) * 16], d=ident[:, :]:
                    e.transpose(o, i, d),
                sem="pe",
            )
            P.wait("vector", "pe", t_t)
            t_cc = P.op("vector", lambda e, o=hT[l][:, tt * 128:(tt + 1) * 128],
                        i=ps[2][:, :].bitcast(BF16)[0:16, 0:128]:
                        e.tensor_copy(o, i), sem="dve")
            P.wait("tensor", "dve", t_cc)
    P.barrier()

    xres = Bf["xres"]
    state = {"out_tgt": [0, 0], "st_x": [0, 0], "y_pe": [0, 0]}

    def extra_loads(oc):
        par = oc % 2
        if state["y_pe"][par]:
            P.wait("gpsimd", "pe", state["y_pe"][par])
        for l in range(4):
            P.dma("gpsimd", bt[par][l][:, :], BTd[l, :, oc * 128:(oc + 1) * 128], "wld")
        if oc > 0:
            P.wait("sync", "dve", state["out_tgt"][(oc - 1) % 2])
        P.dma("sync", bits[0][:, :], bit0d[oc * 128:(oc + 1) * 128, :], "sld")
        t = P.dma("sync", bits[1][:, :], bit1d[oc * 128:(oc + 1) * 128, :], "sld")
        P.wait("tensor", "wld", P.cnt["wld"])
        P.wait("vector", "sld", t)

    def epilogue(oc, grp):
        par = oc % 2
        if state["out_tgt"][par]:
            P.wait("sync", "dve", state["out_tgt"][par])
        if state["st_x"][par]:
            P.wait_all("sync", "st")
        t_x = P.dma("sync", xres[par][:, :], xT[oc * 128:(oc + 1) * 128, :], "sld")
        t_v = None
        for tg in range(NTG):
            if tg > 0:
                P.wait("tensor", "dve", P.cnt["dve"])
            t_y = None
            for l in range(4):
                t_y = P.op(
                    "tensor",
                    lambda e, o=ps[4 + l][:, :], lh=bt[par][l][:, :],
                           r=hT[l][:, tg * 512:(tg + 1) * 512]:
                        e.matmul(o, lh, r, start=True, stop=True),
                    sem=("pe" if l == 3 else None),
                )
            P.wait("scalar", "pe", t_y)
            state["y_pe"][par] = t_y
            bsl = slice(tg * 512, (tg + 1) * 512)
            P.op("scalar", lambda e, o=ysb[:, 0:512], i=ps[4][:, :]: e.copy(o, i))
            P.op("scalar", lambda e, o=ysb[:, 512:1024], i=ps[5][:, :]: e.copy(o, i))
            P.op("scalar", lambda e, o=ysb[:, 1024:1536], i=ps[6][:, :]: e.copy(o, i))
            t_y3 = P.op("scalar", lambda e, o=ysb[:, 1536:2048], i=ps[7][:, :]:
                        e.copy(o, i), sem="act")
            P.wait("vector", "act", t_y3)
            P.op("vector", lambda e, o=scr[:, 1024:1536], m=bits[0][:, bsl],
                 a=ysb[:, 512:1024], b=ysb[:, 0:512]: e.select(o, m, a, b))
            P.op("vector", lambda e, o=scr[:, 1536:2048], m=bits[0][:, bsl],
                 a=ysb[:, 1536:2048], b=ysb[:, 1024:1536]: e.select(o, m, a, b))
            P.op("vector", lambda e, o=scr[:, 0:512], m=bits[1][:, bsl],
                 a=scr[:, 1536:2048], b=scr[:, 1024:1536]: e.select(o, m, a, b))
            P.op("vector", lambda e, o=scr[:, 512:1024], a=scr[:, 0:512], b=grp[tg][:, :]:
                 e.scalar_tensor_tensor(o, a, 8.0, b, OP.mult, OP.add))
            if tg == 0:
                P.wait("vector", "sld", t_x)
            t_v = P.op(
                "vector",
                lambda e, o=xres[par][:, bsl], a=scr[:, 512:1024],
                       s=b0c[:, oc:oc + 1], b=xres[par][:, bsl]:
                    e.scalar_tensor_tensor(o, a, s, b, OP.add, OP.add),
                sem="dve",
            )
        state["out_tgt"][par] = t_v
        P.wait("sync", "dve", t_v)
        state["st_x"][par] = P.dma(
            "sync", xo[oc * 128:(oc + 1) * 128, :], xres[par][:, :], "st")
        return t_v

    emit_ws_loop(P, 32, lambda oc: W0T[oc], Bf, epilogue, ping_pong=False,
                 extra_pe_wait=extra_loads)
    P.wait("sync", "st", P.cnt["st"])
    P.barrier()



def emit_P8(P, consts, Bf, io):
    xT, out = io["xT4"], io["out"]
    scr = Bf["scratch"]
    ident = Bf["ident_f"]
    tin = [scr[:, 0:512], scr[:, 512:1024]]
    tout = [scr[:, 1024:1536], scr[:, 1536:2048]]
    ps = Bf["ps"]

    state = {"tin_free": [0, 0], "tout_free": [0, 0], "ps_free": [0, 0]}
    for tt in range(NTT):
        for d4 in range(8):
            par = d4 % 2
            if state["tin_free"][par]:
                P.wait("gpsimd", "pe", state["tin_free"][par])
            if state["tout_free"][par]:
                P.wait_all("vector", "st")
            t_l = None
            for i in range(4):
                dc = d4 * 4 + i
                t_l = P.dma("gpsimd", tin[par][:, i * 128:(i + 1) * 128],
                            xT[dc * 128:(dc + 1) * 128, tt * 128:(tt + 1) * 128],
                            "xld")
            P.wait("tensor", "xld", t_l)
            if state["ps_free"][par]:
                P.wait("tensor", "dve", state["ps_free"][par])
            t_t = None
            for i in range(4):
                t_t = P.op(
                    "tensor",
                    lambda e, o=ps[par][:, i * 128:(i + 1) * 128],
                           i_=tin[par][:, i * 128:(i + 1) * 128], d=ident[:, :]:
                        e.matmul(o, i_, d, start=True, stop=True),
                    sem=("pe" if i == 3 else None),
                )
            state["tin_free"][par] = t_t
            P.wait("vector", "pe", t_t)
            t_cp = P.op("vector", lambda e, o=tout[par], i=ps[par][:, :]:
                        e.tensor_copy(o, i), sem="dve")
            state["ps_free"][par] = t_cp
            P.wait("sync", "dve", t_cp)
            state["tout_free"][par] = P.dma(
                "sync", out[tt * 128:(tt + 1) * 128, d4 * 512:(d4 + 1) * 512],
                tout[par], "st")
    P.wait("sync", "st", P.cnt["st"])



def _ext(nc, name, shape, dt, out=False):
    kind = "ExternalOutput" if out else "ExternalInput"
    return nc.dram_tensor(name, shape, dt, kind=kind).ap()


IN_SPECS = {
    "xT": ([D, R], F32), "w1_t": ([32, 128, 32 * 128], F32),
    "w2T_r": ([128, 32 * 512], F32), "mask_s": ([4096, 512], F32),
    "b2": ([D], F32), "WT_t": ([32, 128, 32 * 128], F32),
    "gwT_r": ([128, 32 * 8], F32), "ics": ([8, D], F32), "ocs": ([8, D], F32),
    "ffn_bias": ([D], F32), "wqkv_t": ([48, 128, 32 * 128], F32),
    "cos2": ([128, R], F32), "sin2": ([128, R], F32),
    "biascols": ([128, 128], F32), "cmasks": ([4, 8, 128, 512], BF16),
    "wo_t": ([32, 128, 32 * 128], F32), "W0T_t": ([32, 128, 32 * 128], F32),
    "AT_r": ([128, 32 * 64], F32), "BT": ([4, 16, D], F32), "b0": ([D], F32),
    "bit0": ([D, R], U8), "bit1": ([D, R], U8),
}


def build_fused():
    P = Prog()
    nc = P.nc
    io = {k: _ext(nc, k, sh, dt) for k, (sh, dt) in IN_SPECS.items()}
    io["out"] = _ext(nc, "out", [R, D], F32, out=True)
    io["attn_t"] = nc.dram_tensor("attn_t", [4, 128, 32, 128], BF16).ap()
    io["attn_all"] = nc.dram_tensor("attn_all", [32, 128, 32 * 128], BF16,
                                    addr_space="Shared").ap()
    fdbg = os.environ.get("FDBG") == "1"
    for n in ("xT1", "xT2", "xT3", "xT4"):
        io[n] = nc.dram_tensor(n, [D, R], F32,
                               kind=("ExternalOutput" if fdbg else "Internal")).ap()
    if fdbg:
        io["AOT"] = nc.dram_tensor("AOT", [32, 128, R], BF16, kind="ExternalOutput").ap()
    io["QT"] = nc.dram_tensor("QT", [32, 128, R], BF16).ap()
    io["KT"] = nc.dram_tensor("KT", [8, 128, R], BF16).ap()
    io["Vt"] = nc.dram_tensor("Vt", [8, 16, 128, 128], BF16).ap()
    io["KT_pair"] = nc.dram_tensor("KT_pair", [2, 8, 128, R], BF16).ap()
    io["VT_pair"] = nc.dram_tensor("VT_pair", [2, 8, 16, 128, 128], BF16).ap()
    if not fdbg:
        io["AOT"] = nc.dram_tensor("AOT", [32, 128, R], BF16).ap()

    consts = setup_common(P)
    Bf = alloc_shared(P)
    emit_identity(P, Bf["ident_bf"])
    emit_identity(P, Bf["ident_f"])

    emit_P1(P, consts, Bf, io)
    t_cc = P.op("gpsimd", lambda e, i=io["attn_t"], o=io["attn_all"]:
                e.collective_compute(
                    "AllGather", OP.bypass, replica_groups=[list(range(8))],
                    ins=[i[:, :, :, :].rearrange("a b c d -> (a b c) d")],
                    outs=[o[:, :, :].rearrange("a b (c d) -> (a b c) d", d=128)]),
                sem="cc")
    for e in ("tensor", "vector", "scalar", "sync", "gpsimd"):
        P.wait(e, "cc", t_cc)

    emit_P2(P, consts, Bf, io)
    emit_P3(P, consts, Bf, io)
    emit_P4(P, consts, Bf, io)
    pairs = [[0, 1], [2, 3], [4, 5], [6, 7]]
    t_cc = P.op("gpsimd", lambda e, i=io["KT"], o=io["KT_pair"]:
                e.collective_compute(
                    "AllGather", OP.bypass, replica_groups=pairs,
                    ins=[i[:, :, :].rearrange("a b c -> (a b) c")],
                    outs=[o[:, :, :, :].rearrange("a b c d -> (a b c) d")]),
                sem="cc")
    t_cc = P.op("gpsimd", lambda e, i=io["Vt"], o=io["VT_pair"]:
                e.collective_compute(
                    "AllGather", OP.bypass, replica_groups=pairs,
                    ins=[i[:, :, :, :].rearrange("a b c d -> (a b c) d")],
                    outs=[o[:, :, :, :, :].rearrange("a b c d f -> (a b c d) f")]),
                sem="cc")
    for e in ("tensor", "vector", "scalar", "sync", "gpsimd"):
        P.wait(e, "cc", t_cc)

    emit_P5(P, consts, Bf, io)
    emit_P6(P, consts, Bf, io)
    emit_P7(P, consts, Bf, io)
    emit_P8(P, consts, Bf, io)
    return P.finalize()


def _standalone(names, outs, emit, pre_collective=None):
    def build():
        P = Prog()
        nc = P.nc
        io = {}
        for n in names:
            io[n] = _ext(nc, n, *IN_SPECS[n]) if n in IN_SPECS else None
        for n, (sh, dt) in outs.items():
            io[n] = _ext(nc, n, sh, dt, out=True)
        consts = setup_common(P)
        Bf = alloc_shared(P)
        emit_identity(P, Bf["ident_bf"])
        emit_identity(P, Bf["ident_f"])
        emit(P, consts, Bf, io)
        return P.finalize()
    return build


def build_P1():
    P = Prog(); nc = P.nc
    io = {n: _ext(nc, n, *IN_SPECS[n]) for n in ("w1_t", "w2T_r", "mask_s")}
    io["attn_t"] = _ext(nc, "attn_t", [4, 128, 32, 128], BF16, out=True)
    consts = setup_common(P); Bf = alloc_shared(P)
    emit_P1(P, consts, Bf, io)
    return P.finalize()


def build_P2():
    P = Prog(); nc = P.nc
    io = {n: _ext(nc, n, *IN_SPECS[n]) for n in ("xT", "b2")}
    io["attn_all"] = _ext(nc, "attn_all", [32, 128, 32 * 128], BF16)
    io["xT1"] = _ext(nc, "xT1", [D, R], F32, out=True)
    consts = setup_common(P); Bf = alloc_shared(P)
    emit_P2(P, consts, Bf, io)
    return P.finalize()


def build_P3():
    P = Prog(); nc = P.nc
    io = {n: _ext(nc, n, *IN_SPECS[n]) for n in ("WT_t", "gwT_r", "ics", "ocs", "ffn_bias")}
    io["xT1"] = _ext(nc, "xT1", [D, R], F32)
    io["xT2"] = _ext(nc, "xT2", [D, R], F32, out=True)
    consts = setup_common(P); Bf = alloc_shared(P)
    emit_identity(P, Bf["ident_bf"])
    emit_P3(P, consts, Bf, io)
    return P.finalize()


def build_P4():
    P = Prog(); nc = P.nc
    io = {n: _ext(nc, n, *IN_SPECS[n]) for n in ("wqkv_t", "cos2", "sin2")}
    io["xT2"] = _ext(nc, "xT2", [D, R], F32)
    io["QT"] = _ext(nc, "QT", [32, 128, R], BF16, out=True)
    io["KT"] = _ext(nc, "KT", [8, 128, R], BF16, out=True)
    io["Vt"] = _ext(nc, "Vt", [8, 16, 128, 128], BF16, out=True)
    consts = setup_common(P); Bf = alloc_shared(P)
    emit_identity(P, Bf["ident_bf"])
    emit_P4(P, consts, Bf, io)
    return P.finalize()


def build_P5():
    P = Prog(); nc = P.nc
    io = {n: _ext(nc, n, *IN_SPECS[n]) for n in ("biascols", "cmasks")}
    io["QT"] = _ext(nc, "QT", [32, 128, R], BF16)
    io["KT_pair"] = _ext(nc, "KT_pair", [2, 8, 128, R], BF16)
    io["VT_pair"] = _ext(nc, "VT_pair", [2, 8, 16, 128, 128], BF16)
    io["AOT"] = _ext(nc, "AOT", [32, 128, R], BF16, out=True)
    consts = setup_common(P); Bf = alloc_shared(P)
    emit_P5(P, consts, Bf, io)
    return P.finalize()


def build_P6():
    P = Prog(); nc = P.nc
    io = {n: _ext(nc, n, *IN_SPECS[n]) for n in ("wo_t",)}
    io["xT2"] = _ext(nc, "xT2", [D, R], F32)
    io["AOT"] = _ext(nc, "AOT", [32, 128, R], BF16)
    io["xT3"] = _ext(nc, "xT3", [D, R], F32, out=True)
    consts = setup_common(P); Bf = alloc_shared(P)
    emit_P6(P, consts, Bf, io)
    return P.finalize()


def build_P7():
    P = Prog(); nc = P.nc
    io = {n: _ext(nc, n, *IN_SPECS[n]) for n in ("W0T_t", "AT_r", "BT", "b0", "bit0", "bit1")}
    io["xT3"] = _ext(nc, "xT3", [D, R], F32)
    io["xT4"] = _ext(nc, "xT4", [D, R], F32, out=True)
    consts = setup_common(P); Bf = alloc_shared(P)
    emit_identity(P, Bf["ident_bf"])
    emit_P7(P, consts, Bf, io)
    return P.finalize()


def build_P8():
    P = Prog(); nc = P.nc
    io = {}
    io["xT4"] = _ext(nc, "xT4", [D, R], F32)
    io["out"] = _ext(nc, "out", [R, D], F32, out=True)
    consts = setup_common(P); Bf = alloc_shared(P)
    emit_identity(P, Bf["ident_f"])
    emit_P8(P, consts, Bf, io)
    return P.finalize()



def host_prepare(inputs):
    H = {}
    x = np.asarray(inputs["L0_attn_x"], np.float32).reshape(B * S, D)
    H["xT"] = [np.ascontiguousarray(x[rows_for_core(c)].T) for c in range(NCORES)]
    H["w1_t"] = tile_lhsT(np.asarray(inputs["L0_attn_w1"], np.float32)).reshape(32, 128, -1)
    w2T = np.ascontiguousarray(np.asarray(inputs["L0_attn_w2"], np.float32).T)
    mask = np.asarray(inputs["L0_attn_sparse_mask"], np.float32)
    H["w2T_r"] = [
        tile_rhs(np.ascontiguousarray(w2T[:, c * 512:(c + 1) * 512]), 512).reshape(128, -1)
        for c in range(NCORES)
    ]
    H["mask_s"] = [np.ascontiguousarray(mask[:, c * 512:(c + 1) * 512])
                   for c in range(NCORES)]
    H["b2"] = np.asarray(inputs["L0_attn_b2"], np.float32)
    H["WT_t"] = tile_lhsT(np.ascontiguousarray(
        np.asarray(inputs["L0_ffn_weight"], np.float32).T)).reshape(32, 128, -1)
    H["gwT_r"] = tile_rhs(np.ascontiguousarray(
        np.asarray(inputs["L0_ffn_gate_w"], np.float32).T), 8).reshape(128, -1)
    H["ics"] = np.asarray(inputs["L0_ffn_in_channel_scale"], np.float32)
    H["ocs"] = np.asarray(inputs["L0_ffn_out_channel_scale"], np.float32)
    H["ffn_bias"] = np.asarray(inputs["L0_ffn_bias"], np.float32)
    wq = np.asarray(inputs["L1_attn_w_qkv"], np.float32)[:, qkv_perm()]
    H["wqkv_t"] = tile_lhsT(np.ascontiguousarray(wq)).reshape(48, 128, -1)
    H["wo_t"] = tile_lhsT(np.asarray(inputs["L1_attn_w_o"], np.float32)).reshape(32, 128, -1)
    H["W0T_t"] = tile_lhsT(np.ascontiguousarray(
        np.asarray(inputs["L1_ffn_W0"], np.float32).T)).reshape(32, 128, -1)
    A = np.asarray(inputs["L1_ffn_A"], np.float32)
    H["AT_r"] = tile_rhs(np.ascontiguousarray(A.transpose(2, 0, 1).reshape(D, 64)),
                         64).reshape(128, -1)
    H["BT"] = np.ascontiguousarray(np.asarray(inputs["L1_ffn_B"], np.float32)
                                   .transpose(0, 2, 1))
    H["b0"] = np.asarray(inputs["L1_ffn_b0"], np.float32)
    ropes = [rope_tables(c) for c in range(NCORES)]
    H["cos2"] = [r[0] for r in ropes]
    H["sin2"] = [r[1] for r in ropes]
    cd = [causal_data(c) for c in range(NCORES)]
    H["biascols"] = [c[0] for c in cd]
    H["cmasks"] = [c[1] for c in cd]
    bit0, bit1 = dropout_bits()
    bit0 = bit0.reshape(B * S, D)
    bit1 = bit1.reshape(B * S, D)
    H["bit0"] = [np.ascontiguousarray(bit0[rows_for_core(c)].T) for c in range(NCORES)]
    H["bit1"] = [np.ascontiguousarray(bit1[rows_for_core(c)].T) for c in range(NCORES)]
    return H


def run(nc, in_maps):
    return run_bass_kernel_spmd(nc, in_maps, core_ids=list(range(NCORES))).results


PER_CORE = {"xT", "w2T_r", "mask_s", "cos2", "sin2", "biascols", "cmasks",
            "bit0", "bit1"}


def kernel(**inputs):
    H = host_prepare(inputs)
    nc = build_fused()
    in_maps = [
        {k: (H[k][c] if k in PER_CORE else H[k]) for k in IN_SPECS}
        for c in range(NCORES)
    ]
    res = run(nc, in_maps)
    out = np.empty((B * S, D), np.float32)
    for c in range(NCORES):
        out[rows_for_core(c)] = res[c]["out"]
    return out.reshape(B, S, D)


def kernel_unfused(**inputs):
    H = host_prepare(inputs)
    r1 = run(build_P1(), [
        {"w1_t": H["w1_t"], "w2T_r": H["w2T_r"][c], "mask_s": H["mask_s"][c]}
        for c in range(NCORES)
    ])
    attn_all = np.ascontiguousarray(
        np.concatenate([np.asarray(r1[c]["attn_t"]) for c in range(NCORES)],
                       axis=0)).reshape(32, 128, 32 * 128)
    r2 = run(build_P2(), [
        {"xT": H["xT"][c], "attn_all": attn_all, "b2": H["b2"]}
        for c in range(NCORES)
    ])
    r3 = run(build_P3(), [
        {"xT1": r2[c]["xT1"], "WT_t": H["WT_t"], "gwT_r": H["gwT_r"],
         "ics": H["ics"], "ocs": H["ocs"], "ffn_bias": H["ffn_bias"]}
        for c in range(NCORES)
    ])
    r4 = run(build_P4(), [
        {"xT2": r3[c]["xT2"], "wqkv_t": H["wqkv_t"],
         "cos2": H["cos2"][c], "sin2": H["sin2"][c]}
        for c in range(NCORES)
    ])
    r5 = run(build_P5(), [
        {"QT": r4[c]["QT"],
         "KT_pair": np.stack([r4[c // 2 * 2]["KT"], r4[c // 2 * 2 + 1]["KT"]]),
         "VT_pair": np.stack([r4[c // 2 * 2]["Vt"], r4[c // 2 * 2 + 1]["Vt"]]),
         "biascols": H["biascols"][c], "cmasks": H["cmasks"][c]}
        for c in range(NCORES)
    ])
    r6 = run(build_P6(), [
        {"xT2": r3[c]["xT2"], "AOT": r5[c]["AOT"], "wo_t": H["wo_t"]}
        for c in range(NCORES)
    ])
    r7 = run(build_P7(), [
        {"xT3": r6[c]["xT3"], "W0T_t": H["W0T_t"], "AT_r": H["AT_r"],
         "BT": H["BT"], "b0": H["b0"], "bit0": H["bit0"][c], "bit1": H["bit1"][c]}
        for c in range(NCORES)
    ])
    r8 = run(build_P8(), [{"xT4": r7[c]["xT4"]} for c in range(NCORES)])
    out = np.empty((B * S, D), np.float32)
    for c in range(NCORES):
        out[rows_for_core(c)] = r8[c]["out"]
    return out.reshape(B, S, D)


# version 08
# speedup vs baseline: 1.0183x; 1.0183x over previous
import contextlib
import math
import os
import numpy as np
import ml_dtypes

import concourse.bass as bass
import concourse.mybir as mybir
from concourse.bass_utils import run_bass_kernel_spmd

F32 = mybir.dt.float32
BF16 = mybir.dt.bfloat16
U8 = mybir.dt.uint8
AF = mybir.ActivationFunctionType
OP = mybir.AluOpType
AX = mybir.AxisListType.X

D = 4096
B = 4
S = 4096
R = 2048
NDC = D // 128
NTT = R // 128
NTG = R // 512
NCORES = 8
ATT_SCALE = 1.0 / math.sqrt(128.0)
EPS = 1e-5

GROUPS_EVEN = [0, 3, 4, 7]
GROUPS_ODD = [1, 2, 5, 6]
BOUND = [8, 16, 24, 32]
CAND = [list(range(b - 8, b)) for b in BOUND]

BF = ml_dtypes.bfloat16


def groups_for_core(c):
    return GROUPS_EVEN if c % 2 == 0 else GROUPS_ODD


def rows_for_core(c):
    b = c // 2
    return np.concatenate(
        [b * S + 512 * g + np.arange(512) for g in groups_for_core(c)]
    )



def tile_lhsT(W):
    K, M = W.shape
    return np.ascontiguousarray(
        W.reshape(K // 128, 128, M // 128, 128).transpose(2, 1, 0, 3)
    )


def tile_rhs(Wr, n):
    K = Wr.shape[0]
    return np.ascontiguousarray(Wr.reshape(K // 128, 128, n).transpose(1, 0, 2))


def qkv_perm():
    perm = []
    for p in range(16):
        h0, h1 = 2 * p, 2 * p + 1
        perm.extend(h0 * 128 + np.arange(0, 128, 2))
        perm.extend(h1 * 128 + np.arange(0, 128, 2))
        perm.extend(h0 * 128 + np.arange(1, 128, 2))
        perm.extend(h1 * 128 + np.arange(1, 128, 2))
    for p in range(4):
        h0, h1 = 2 * p, 2 * p + 1
        perm.extend(4096 + h0 * 128 + np.arange(0, 128, 2))
        perm.extend(4096 + h1 * 128 + np.arange(0, 128, 2))
        perm.extend(4096 + h0 * 128 + np.arange(1, 128, 2))
        perm.extend(4096 + h1 * 128 + np.arange(1, 128, 2))
    perm.extend(4096 + 1024 + np.arange(1024))
    return np.asarray(perm)


def rope_tables(c):
    pos = (rows_for_core(c) % S).astype(np.float64)
    inv_freq = 1.0 / (10000.0 ** (np.arange(0, 128, 2, dtype=np.float64) / 128.0))
    ang = inv_freq[:, None] * pos[None, :]
    cos = np.cos(ang).astype(np.float32)
    sin = np.sin(ang).astype(np.float32)
    return (
        np.ascontiguousarray(np.concatenate([cos, cos], axis=0)),
        np.ascontiguousarray(np.concatenate([sin, sin], axis=0)),
    )


def causal_data(c):
    groups = groups_for_core(c)
    biascols = np.zeros((128, 4 * 32), np.float32)
    masks = np.zeros((4, 8, 128, 512), np.float32)
    for j, g in enumerate(groups):
        qbase = 512 * g
        qmax = qbase + 511
        for kc in range(BOUND[j]):
            kpos = 128 * kc + np.arange(128)
            biascols[:, j * 32 + kc] = np.where(kpos > qmax, -1e30, 0.0)
        for ci, kc in enumerate(CAND[j]):
            kpos = 128 * kc + np.arange(128)
            qpos = qbase + np.arange(512)
            masks[j, ci] = np.where(kpos[:, None] > qpos[None, :], -1e30, 0.0)
    return biascols, masks.astype(BF)


def dropout_bits():
    import jax
    with jax.default_device(jax.devices("cpu")[0]):
        key = jax.random.key(42, impl="threefry2x32")
        u = np.asarray(jax.random.uniform(key, (B, S, D)))
    sel = (u >= 0.25).astype(np.uint8) + (u >= 0.5) + (u >= 0.75)
    return (sel & 1).astype(np.uint8), (sel >> 1).astype(np.uint8)



ENGINES = ("sync", "tensor", "vector", "scalar", "gpsimd")


class Prog:
    def __init__(self):
        self.nc = bass.Bass("TRN2", target_bir_lowering=False)
        self.ops = {e: [] for e in ENGINES}
        self.stack = contextlib.ExitStack()
        self.sems = {}
        self.cnt = {}
        self.bar_cnt = 0

    def sem(self, name):
        self.sems[name] = self.stack.enter_context(self.nc.semaphore(name))
        self.cnt[name] = 0
        return self.sems[name]

    def sbuf(self, name, shape, dt):
        return self.stack.enter_context(self.nc.sbuf_tensor(name, shape, dt))

    def psum(self, name, shape, dt=F32):
        return self.stack.enter_context(self.nc.psum_tensor(name, shape, dt))

    def emit(self, eng, fn):
        self.ops[eng].append(fn)

    def wait(self, eng, sem, val):
        h = self.sems[sem]
        self.emit(eng, lambda e, h=h, v=val: e.wait_ge(h, v))

    def wait_all(self, eng, sem):
        self.wait(eng, sem, self.cnt[sem])

    def op(self, eng, fn, sem=None, inc=1):
        if sem is None:
            self.emit(eng, fn)
            return None
        self.cnt[sem] += inc
        h = self.sems[sem]
        self.emit(eng, lambda e, fn=fn, h=h, n=inc: fn(e).then_inc(h, n))
        return self.cnt[sem]

    def dma(self, q, out, in_, sem, **kw):
        return self.op(
            q, lambda e, o=out, i=in_, kw=kw: e.dma_start(out=o, in_=i, **kw),
            sem=sem, inc=16,
        )

    def barrier(self, engines=ENGINES):
        if "bar" not in self.sems:
            self.sem("bar")
        if not hasattr(self, "bar_scr"):
            self.bar_scr = self.sbuf("bar_scr", [1, 2], F32)
        h = self.sems["bar"]
        scr = self.bar_scr
        self.bar_cnt += len(engines)
        tgt = self.bar_cnt
        for e in engines:
            if e in ("vector", "gpsimd"):
                self.emit(e, lambda eng, h=h: eng.engine_nop().then_inc(h, 1))
            elif e == "scalar":
                self.emit(e, lambda eng, h=h, a=scr[0:1, 0:1], b=scr[0:1, 1:2]:
                          eng.copy(b, a).then_inc(h, 1))
            else:
                self.emit(e, lambda eng, h=h: eng.sem_inc(h, 1))
        for e in engines:
            self.emit(e, lambda eng, h=h, v=tgt: eng.wait_ge(h, v))

    def finalize(self):
        blk = self.stack.enter_context(self.nc.Block())
        for e in ENGINES:
            getattr(blk, e)(lambda eng, ops=self.ops[e]: [f(eng) for f in ops])
        self.stack.close()
        return self.nc


def setup_common(P):
    for s in ("wld", "xld", "sld", "st", "pe", "act", "dve", "gp", "cc"):
        P.sem(s)
    P.sem("bar")
    c = {}
    c["ones_bf"] = P.sbuf("ones_bf", [128, 1], BF16)
    c["ones1_f"] = P.sbuf("ones1_f", [1, 128], F32)
    c["eps"] = P.sbuf("eps", [1, 1], F32)
    P.op("gpsimd", lambda e, t=c["ones_bf"]: e.memset(t[:, :], 1.0))
    P.op("gpsimd", lambda e, t=c["ones1_f"]: e.memset(t[:, :], 1.0))
    P.op("gpsimd", lambda e, t=c["eps"]: e.memset(t[:, :], EPS), sem="gp")
    P.barrier()
    return c


def emit_identity(P, ident):
    P.op("gpsimd", lambda e, t=ident: e.memset(t[:, :], 0.0))
    P.op(
        "gpsimd",
        lambda e, t=ident: e.affine_select(
            out=t[:, :], in_=t[:, :], compare_op=OP.not_equal,
            fill=1.0, base=0, pattern=[[-1, 128]], channel_multiplier=1,
        ),
        sem="gp",
    )
    P.barrier()
    return ident



def alloc_shared(P):
    Bf = {}
    Bf["xln"] = P.sbuf("xln", [128, NDC * R], BF16)
    Bf["wring"] = [P.sbuf(f"wring{i}", [128, 4096], BF16) for i in range(2)]
    Bf["scratch"] = P.sbuf("scratch", [128, R], F32)
    Bf["m_bc"] = P.sbuf("m_bc", [128, R], BF16)
    Bf["r_bc"] = P.sbuf("r_bc", [128, R], BF16)
    Bf["bstmp"] = P.sbuf("bstmp", [128, 2 * R], BF16)
    Bf["xres"] = [P.sbuf(f"xres{i}", [128, R], F32) for i in range(2)]
    Bf["m_row"] = Bf["scratch"][0:1, 0:R]
    Bf["r_row"] = Bf["xres"][0][0:1, 0:R]
    Bf["t0"] = Bf["m_bc"][0:1, 0:512]
    Bf["t1"] = Bf["r_bc"][0:1, 0:512]
    Bf["bias32"] = P.sbuf("bias32", [128, 32], F32)
    Bf["hT3"] = P.sbuf("hT3", [16, R], BF16)
    bt_all = [P.sbuf(f"bt_all{i}", [128, 256], BF16) for i in range(2)]
    Bf["bt"] = [
        [bt_all[i][0:16, 0:128], bt_all[i][32:48, 0:128],
         bt_all[i][64:80, 0:128], bt_all[i][0:16, 128:256]]
        for i in range(2)
    ]
    Bf["bits"] = [P.sbuf(f"bit{b}_s", [128, R], U8) for b in range(2)]
    Bf["ident_bf"] = P.sbuf("ident_bf", [128, 128], BF16)
    Bf["ident_f"] = P.sbuf("ident_f", [128, 128], F32)
    Bf["ps"] = [P.psum(f"ps{i}", [128, 512], F32) for i in range(8)]
    return Bf



def emit_stats_xln(P, xT, bufs, consts, nsq=2):
    xln = bufs["xln"]
    sq = bufs["bstmp"]
    scr = bufs["scratch"]
    m_bc, r_bc = bufs["m_bc"], bufs["r_bc"]
    ps = bufs["ps"]

    for dc in range(NDC):
        t_ld = P.dma(
            "gpsimd", xln[:, dc * R:(dc + 1) * R],
            xT[dc * 128:(dc + 1) * 128, :], "xld",
        )
        P.wait("scalar", "xld", t_ld)
        sqs_ = (dc % nsq) * R
        t_sq = P.op(
            "scalar",
            lambda e, i=xln[:, dc * R:(dc + 1) * R], o=sq[:, sqs_:sqs_ + R]:
                e.activation(o, i, AF.Square),
            sem="act",
        )
        P.wait("tensor", "act", t_sq)
        for tg in range(NTG):
            P.op(
                "tensor",
                lambda e, o=ps[tg][0:1, :], l=consts["ones_bf"][:, :],
                       r=xln[:, dc * R + tg * 512: dc * R + (tg + 1) * 512],
                       st=(dc == 0), sp=(dc == NDC - 1):
                    e.matmul(o, l, r, start=st, stop=sp),
            )
        t_pe = None
        for tg in range(NTG):
            t_pe = P.op(
                "tensor",
                lambda e, o=ps[4 + tg][0:1, :], l=consts["ones_bf"][:, :],
                       r=sq[:, sqs_ + tg * 512:sqs_ + (tg + 1) * 512],
                       st=(dc == 0), sp=(dc == NDC - 1):
                    e.matmul(o, l, r, start=st, stop=sp),
                sem="pe",
            )
        if dc + nsq < NDC:
            P.wait("scalar", "pe", t_pe)

    P.wait("vector", "pe", P.cnt["pe"])
    m_row = bufs["m_row"]
    r_row = bufs["r_row"]
    t0 = bufs["t0"]
    t1 = bufs["t1"]
    for tg in range(NTG):
        sl = slice(tg * 512, (tg + 1) * 512)
        P.op("vector", lambda e, o=m_row[:, sl], i=ps[tg][0:1, :]:
             e.tensor_scalar_mul(o, i, 1.0 / D))
        P.op("vector", lambda e, o=r_row[:, sl], i=ps[4 + tg][0:1, :]:
             e.tensor_scalar_mul(o, i, 1.0 / D))
        P.op("vector", lambda e, o=t0, a=m_row[:, sl]:
             e.tensor_mul(o, a, a))
        t_v = P.op("vector", lambda e, o=r_row[:, sl], a=r_row[:, sl], b=t0:
                   e.tensor_tensor(o, a, b, OP.subtract), sem="dve")
        P.wait("scalar", "dve", t_v)
        t_s = P.op("scalar", lambda e, o=t1, i=r_row[:, sl], b=consts["eps"][:, :]:
                   e.activation(o, i, AF.Sqrt, bias=b), sem="act")
        P.wait("vector", "act", t_s)
        P.op("vector", lambda e, o=r_row[:, sl], i=t1: e.reciprocal(o, i), sem="dve")

    P.wait("tensor", "dve", P.cnt["dve"])
    for i, (row, bc) in enumerate([(m_row, m_bc), (r_row, r_bc)]):
        for tg in range(NTG):
            sl = slice(tg * 512, (tg + 1) * 512)
            if i > 0 or tg > 0:
                P.wait("tensor", "dve", P.cnt["dve"])
            t_pe = P.op(
                "tensor",
                lambda e, o=ps[i % 2][:, :], l=consts["ones1_f"][:, :], r=row[:, sl]:
                    e.matmul(o, l, r, start=True, stop=True),
                sem="pe",
            )
            P.wait("vector", "pe", t_pe)
            P.op("vector", lambda e, o=bc[:, sl], i=ps[i % 2][:, :]:
                 e.tensor_copy(o, i), sem="dve")

    for dc in range(NDC):
        sl = slice(dc * R, (dc + 1) * R)
        P.op("vector", lambda e, o=sq[:, 0:R], a=xln[:, sl], b=m_bc[:, :]:
             e.tensor_tensor(o, a, b, OP.subtract))
        P.op("vector", lambda e, o=xln[:, sl], a=sq[:, 0:R], b=r_bc[:, :]:
             e.tensor_tensor(o, a, b, OP.mult), sem="dve")
    P.barrier()



def emit_ws_loop(P, oc_count, w_src, bufs, epilogue, rhs_buf=None,
                 ping_pong=True, extra_pe_wait=None):
    xln = rhs_buf if rhs_buf is not None else bufs["xln"]
    ps = bufs["ps"]
    groups = [ps[0:4], ps[4:8]] if ping_pong else [ps[0:4], ps[0:4]]
    free_tgt = [0, 0]
    wslot_free = [0, 0]
    for oc in range(oc_count):
        par = oc % 2
        if wslot_free[par]:
            P.wait("gpsimd", "pe", wslot_free[par])
        t_w = P.dma("gpsimd", bufs["wring"][par][:, :], w_src(oc), "wld")
        P.wait("tensor", "wld", t_w)
        ft = free_tgt[par] if ping_pong else max(free_tgt)
        if ft:
            P.wait("tensor", "dve", ft)
        if extra_pe_wait is not None:
            extra_pe_wait(oc)
        grp = groups[par]
        t_pe = None
        for dc in range(NDC):
            for tg in range(NTG):
                t_pe = P.op(
                    "tensor",
                    lambda e, o=grp[tg][:, :],
                           l=bufs["wring"][par][:, dc * 128:(dc + 1) * 128],
                           r=xln[:, dc * R + tg * 512: dc * R + (tg + 1) * 512],
                           st=(dc == 0), sp=(dc == NDC - 1):
                        e.matmul(o, l, r, start=st, stop=sp),
                    sem=("pe" if (dc == NDC - 1 and tg == NTG - 1) else None),
                )
        wslot_free[par] = P.cnt["pe"]
        free_tgt[par] = epilogue(oc, grp)



def emit_P1(P, consts, Bf, io):
    w1, w2T, maskd, out = io["w1_t"], io["w2T_r"], io["mask_s"], io["attn_t"]
    w2sb = Bf["xln"][:, 0:32 * 512]
    wring = Bf["wring"]
    scr = Bf["scratch"]
    msk = [scr[:, 0:512], scr[:, 512:1024]]
    och = [Bf["bstmp"][:, 0:512], Bf["bstmp"][:, 512:1024]]
    ps = Bf["ps"]

    t_w2 = P.dma("gpsimd", w2sb, w2T[:, :], "xld")
    P.wait("tensor", "xld", t_w2)

    wfree = [0, 0]
    ofree = [0, 0]
    st_free = [0, 0]
    for sc in range(32):
        par = sc % 2
        if wfree[par]:
            P.wait("gpsimd", "pe", wfree[par])
        t_w = P.dma("gpsimd", wring[par][:, :], w1[sc], "wld")
        t_m = P.dma("sync", msk[par], maskd[sc * 128:(sc + 1) * 128, :], "sld")
        P.wait("tensor", "wld", t_w)
        if ofree[par]:
            P.wait("tensor", "dve", ofree[par])
        t_pe = None
        for dc in range(NDC):
            t_pe = P.op(
                "tensor",
                lambda e, o=ps[par][:, :], l=wring[par][:, dc * 128:(dc + 1) * 128],
                       r=w2sb[:, dc * 512:(dc + 1) * 512],
                       st=(dc == 0), sp=(dc == NDC - 1):
                    e.matmul(o, l, r, start=st, stop=sp),
                sem=("pe" if dc == NDC - 1 else None),
            )
        wfree[par] = t_pe
        P.wait("vector", "pe", t_pe)
        P.wait("vector", "sld", t_m)
        if st_free[par]:
            P.wait_all("vector", "st")
        t_v = P.op("vector", lambda e, o=och[par], a=ps[par][:, :], b=msk[par]:
                   e.tensor_tensor(o, a, b, OP.mult), sem="dve")
        ofree[par] = t_v
        P.wait("sync", "dve", t_v)
        for tcl in range(4):
            st_free[par] = P.dma("sync", out[tcl, :, sc, :],
                                 och[par][:, tcl * 128:(tcl + 1) * 128], "st")
    P.wait("sync", "st", P.cnt["st"])
    P.barrier()



def emit_P2(P, consts, Bf, io):
    xT, attn, b2d, xo = io["xT"], io["attn_all"], io["b2"], io["xT1"]
    b2c = Bf["bias32"]
    P.dma("gpsimd", b2c[:, :], b2d.rearrange("(c p) -> p c", p=128), "xld",
          allow_slow_non_contiguous=True)
    emit_stats_xln(P, xT, Bf, consts)

    gel = Bf["bstmp"]
    xres = Bf["xres"]
    state = {"out_tgt": [0, 0], "st_x": [0, 0]}

    def epilogue(oc, grp):
        par = oc % 2
        if state["out_tgt"][par]:
            P.wait("sync", "dve", state["out_tgt"][par])
        if state["st_x"][par]:
            P.wait_all("sync", "st")
        t_x = P.dma("sync", xres[par][:, :], xT[oc * 128:(oc + 1) * 128, :], "sld")
        t_pe = P.cnt["pe"]
        P.wait("scalar", "pe", t_pe)
        if state["out_tgt"][par]:
            P.wait("scalar", "dve", state["out_tgt"][par])
        t_act = None
        for tg in range(NTG):
            t_act = P.op(
                "scalar",
                lambda e, o=gel[:, (par * R) + tg * 512:(par * R) + (tg + 1) * 512],
                       i=grp[tg][:, :], b=b2c[:, oc:oc + 1]:
                    e.activation(o, i, AF.Gelu, bias=b),
                sem="act",
            )
        P.wait("vector", "act", t_act)
        P.wait("vector", "sld", t_x)
        t_v = P.op(
            "vector",
            lambda e, o=xres[par][:, :], a=gel[:, par * R:(par + 1) * R], b=xres[par][:, :]:
                e.tensor_tensor(o, a, b, OP.add),
            sem="dve",
        )
        state["out_tgt"][par] = t_v
        P.wait("sync", "dve", t_v)
        state["st_x"][par] = P.dma(
            "sync", xo[oc * 128:(oc + 1) * 128, :], xres[par][:, :], "st")
        return t_v

    emit_ws_loop(P, 32, lambda oc: attn[oc], Bf, epilogue)
    P.wait("sync", "st", P.cnt["st"])
    P.barrier()



def emit_P3(P, consts, Bf, io):
    xT, WT, gwT = io["xT1"], io["WT_t"], io["gwT_r"]
    icsd, ocsd, biasd, xo = io["ics"], io["ocs"], io["ffn_bias"], io["xT2"]
    bias_c = Bf["bias32"]
    scr = Bf["scratch"]
    xln = Bf["xln"]
    ps = Bf["ps"]
    ident = Bf["ident_bf"]
    routT = Bf["m_bc"][0:8, 0:R]
    gw_sb = Bf["r_bc"][:, 0:256]
    ics_s = [Bf["r_bc"][0:8, 256 + i * 128:256 + (i + 1) * 128] for i in range(2)]
    ocs_s = [Bf["r_bc"][0:8, 512 + i * 128:512 + (i + 1) * 128] for i in range(2)]
    rout_b = Bf["r_bc"][:, 768:776]
    rout = scr[:, 0:8]
    rsml = scr[:, 8:12]

    P.dma("gpsimd", bias_c[:, :], biasd.rearrange("(c p) -> p c", p=128), "xld",
          allow_slow_non_contiguous=True)
    emit_stats_xln(P, xT, Bf, consts)

    t_gw = P.dma("gpsimd", gw_sb, gwT[:, :], "wld")
    P.wait("tensor", "wld", t_gw)
    lg_free = [0, 0]
    for tt in range(NTT):
        par = tt % 2
        if lg_free[par]:
            P.wait("tensor", "act", lg_free[par])
        t_pe = None
        for dc in range(NDC):
            t_pe = P.op(
                "tensor",
                lambda e, o=ps[par][:, 0:8],
                       l=xln[:, dc * R + tt * 128: dc * R + (tt + 1) * 128],
                       r=gw_sb[:, dc * 8:(dc + 1) * 8],
                       st=(dc == 0), sp=(dc == NDC - 1):
                    e.matmul(o, l, r, start=st, stop=sp),
                sem=("pe" if dc == NDC - 1 else None),
            )
        P.wait("vector", "pe", t_pe)
        t_mx = P.op("vector", lambda e, o=rsml[:, 1:2], i=ps[par][:, 0:8]:
                    e.reduce_max(o, i, axis=AX, negate=True), sem="dve")
        P.wait("scalar", "dve", t_mx)
        t_a = P.op("scalar", lambda e, o=rout, i=ps[par][:, 0:8], b=rsml[:, 1:2],
                   a_=rsml[:, 2:3]:
                   e.activation(o, i, AF.Exp, bias=b, accum_out=a_), sem="act")
        lg_free[par] = t_a
        P.wait("vector", "act", t_a)
        t_rc = P.op("vector", lambda e, o=rsml[:, 2:3], i=rsml[:, 2:3]:
                    e.reciprocal(o, i), sem="dve")
        P.wait("scalar", "dve", t_rc)
        t_cb = P.op("scalar", lambda e, o=rsml[:, 3:4], i=rsml[:, 2:3]:
                    e.copy(o, i), sem="act")
        P.wait("vector", "act", t_cb)
        t_cp = P.op("vector", lambda e, o=rout_b, i=rout, s=rsml[:, 3:4]:
                    e.tensor_scalar(o, i, s, None, OP.mult), sem="dve")
        P.wait("tensor", "dve", t_cp)
        t_t = P.op("tensor", lambda e, o=ps[2][:, :].bitcast(BF16)[0:8, 0:128],
                   i=rout_b, d=ident[:, :]:
                   e.transpose(o, i, d), sem="pe")
        P.wait("vector", "pe", t_t)
        P.op("vector", lambda e, o=routT[:, tt * 128:(tt + 1) * 128],
             i=ps[2][:, :].bitcast(BF16)[0:8, 0:128]:
             e.tensor_copy(o, i), sem="dve")
    P.barrier()

    for dc in range(NDC):
        par_i = dc % 2
        if dc >= 2:
            P.wait("gpsimd", "pe", P.cnt["pe"] - 1)
        t_i = P.dma("gpsimd", ics_s[par_i], icsd[:, dc * 128:(dc + 1) * 128], "wld")
        P.wait("tensor", "wld", t_i)
        t_pe = None
        for tg in range(NTG):
            t_pe = P.op(
                "tensor",
                lambda e, o=ps[4 + tg][:, :], l=ics_s[par_i],
                       r=routT[:, tg * 512:(tg + 1) * 512]:
                    e.matmul(o, l, r, start=True, stop=True),
                sem=("pe" if tg == NTG - 1 else None),
            )
        P.wait("vector", "pe", t_pe)
        t_v = None
        for tg in range(NTG):
            sl = slice(dc * R + tg * 512, dc * R + (tg + 1) * 512)
            t_v = P.op(
                "vector",
                lambda e, o=xln[:, sl], a=xln[:, sl], b=ps[4 + tg][:, :]:
                    e.tensor_tensor(o, a, b, OP.mult),
                sem="dve",
            )
        P.wait("tensor", "dve", t_v)
    P.barrier()

    xres = Bf["xres"]
    state = {"out_tgt": [0, 0], "st_x": [0, 0], "os_pe": [0, 0]}

    def epilogue(oc, grp):
        par = oc % 2
        if state["out_tgt"][par]:
            P.wait("sync", "dve", state["out_tgt"][par])
        if state["st_x"][par]:
            P.wait_all("sync", "st")
        t_x = P.dma("sync", xres[par][:, :], xT[oc * 128:(oc + 1) * 128, :], "sld")
        if state["os_pe"][par]:
            P.wait("gpsimd", "pe", state["os_pe"][par])
        t_oc = P.dma("gpsimd", ocs_s[par], ocsd[:, oc * 128:(oc + 1) * 128], "wld")
        P.wait("tensor", "wld", t_oc)
        t_v = None
        for tg in range(NTG):
            if tg > 0:
                P.wait("tensor", "dve", P.cnt["dve"])
            t_os = P.op(
                "tensor",
                lambda e, o=ps[4][:, :], l=ocs_s[par],
                       r=routT[:, tg * 512:(tg + 1) * 512]:
                    e.matmul(o, l, r, start=True, stop=True),
                sem="pe",
            )
            P.wait("scalar", "pe", t_os)
            t_osc = P.op("scalar", lambda e, o=scr[:, 0:512], i=ps[4][:, :]:
                         e.copy(o, i), sem="act")
            P.wait("vector", "act", t_osc)
            P.op("vector", lambda e, o=scr[:, 512:1024], a=grp[tg][:, :], b=scr[:, 0:512]:
                 e.tensor_tensor(o, a, b, OP.mult))
            if tg == 0:
                P.wait("vector", "sld", t_x)
            t_v = P.op(
                "vector",
                lambda e, o=xres[par][:, tg * 512:(tg + 1) * 512], a=scr[:, 512:1024],
                       s=bias_c[:, oc:oc + 1], b=xres[par][:, tg * 512:(tg + 1) * 512]:
                    e.scalar_tensor_tensor(o, a, s, b, OP.add, OP.add),
                sem="dve",
            )
        state["out_tgt"][par] = t_v
        state["os_pe"][par] = P.cnt["pe"]
        P.wait("sync", "dve", t_v)
        state["st_x"][par] = P.dma(
            "sync", xo[oc * 128:(oc + 1) * 128, :], xres[par][:, :], "st")
        return t_v

    def extra_sign(oc):
        par = oc % 2
        t_w = P.cnt["wld"]
        P.wait("scalar", "wld", t_w)
        t_s = P.op("scalar", lambda e, t=Bf["wring"][par][:, :]:
                   e.sign(t[:, :], t[:, :]), sem="act")
        P.wait("tensor", "act", t_s)

    emit_ws_loop(P, 32, lambda oc: WT[oc], Bf, epilogue, ping_pong=False,
                 extra_pe_wait=extra_sign)
    P.wait("sync", "st", P.cnt["st"])
    P.barrier()



def emit_P4(P, consts, Bf, io):
    xT, wq = io["xT2"], io["wqkv_t"]
    cosd, sind = io["cos2"], io["sin2"]
    QT, KT, VT = io["QT"], io["KT"], io["Vt"]
    ident = Bf["ident_bf"]
    emit_stats_xln(P, xT, Bf, consts)

    xln = Bf["xln"]
    ps = Bf["ps"]
    scr = Bf["scratch"]
    bstmp = Bf["bstmp"]
    wring = Bf["wring"]
    cos2 = Bf["m_bc"][:, :]
    sin2 = Bf["r_bc"][:, :]
    t_cs = P.dma("gpsimd", cos2, cosd[:, :], "xld")
    t_cs = P.dma("gpsimd", sin2, sind[:, :], "xld")
    P.wait("vector", "xld", t_cs)

    state = {"free": 0, "wfree": [0, 0], "st_qk": 0}
    for pair in range(20):
        oc0, oc1 = 2 * pair, 2 * pair + 1
        for k, oc in enumerate((oc0, oc1)):
            par = k
            if state["wfree"][par]:
                P.wait("gpsimd", "pe", state["wfree"][par])
            t_w = P.dma("gpsimd", wring[par][:, :], wq[oc], "wld")
            P.wait("tensor", "wld", t_w)
        if state["free"]:
            P.wait("tensor", "dve", state["free"])
        t_pe = None
        for k in range(2):
            for dc in range(NDC):
                for tg in range(NTG):
                    t_pe = P.op(
                        "tensor",
                        lambda e, o=ps[4 * k + tg][:, :],
                               l=wring[k][:, dc * 128:(dc + 1) * 128],
                               r=xln[:, dc * R + tg * 512: dc * R + (tg + 1) * 512],
                               st=(dc == 0), sp=(dc == NDC - 1):
                            e.matmul(o, l, r, start=st, stop=sp),
                        sem=("pe" if (dc == NDC - 1 and tg == NTG - 1) else None),
                    )
        state["wfree"] = [t_pe, t_pe]
        P.wait("vector", "pe", t_pe)
        if state["st_qk"]:
            P.wait_all("vector", "st")
        x1 = [ps[tg][:, :] for tg in range(4)]
        x2 = [ps[4 + tg][:, :] for tg in range(4)]
        t_v = None
        for tg in range(NTG):
            sl = slice(tg * 512, (tg + 1) * 512)
            P.op("vector", lambda e, o=scr[:, 0:512], a=x1[tg], b=cos2[:, sl]:
                 e.tensor_tensor(o, a, b, OP.mult))
            P.op("vector", lambda e, o=scr[:, 512:1024], a=x2[tg], b=sin2[:, sl]:
                 e.tensor_tensor(o, a, b, OP.mult))
            P.op("vector", lambda e, o=bstmp[:, sl], a=scr[:, 0:512], b=scr[:, 512:1024]:
                 e.tensor_tensor(o, a, b, OP.subtract))
            P.op("vector", lambda e, o=scr[:, 0:512], a=x1[tg], b=sin2[:, sl]:
                 e.tensor_tensor(o, a, b, OP.mult))
            P.op("vector", lambda e, o=scr[:, 512:1024], a=x2[tg], b=cos2[:, sl]:
                 e.tensor_tensor(o, a, b, OP.mult))
            t_v = P.op("vector", lambda e, o=bstmp[:, R + tg * 512:R + (tg + 1) * 512],
                       a=scr[:, 0:512], b=scr[:, 512:1024]:
                       e.tensor_tensor(o, a, b, OP.add), sem="dve")
        state["free"] = t_v
        P.wait("sync", "dve", t_v)
        if pair < 16:
            P.dma("sync", QT[oc0], bstmp[:, 0:R], "st")
            state["st_qk"] = P.dma("sync", QT[oc1], bstmp[:, R:2 * R], "st")
        else:
            P.dma("sync", KT[oc0 - 32], bstmp[:, 0:R], "st")
            state["st_qk"] = P.dma("sync", KT[oc1 - 32], bstmp[:, R:2 * R], "st")
    P.barrier()

    vfree = [0, 0]
    for ocv in range(8):
        oc = 40 + ocv
        par = ocv % 2
        if vfree[par]:
            P.wait("gpsimd", "pe", vfree[par])
        t_w = P.dma("gpsimd", wring[par][:, :], wq[oc], "wld")
        P.wait("tensor", "wld", t_w)
        t_pe = None
        for dc in range(NDC):
            for tg in range(NTG):
                t_pe = P.op(
                    "tensor",
                    lambda e, o=ps[tg][:, :],
                           l=wring[par][:, dc * 128:(dc + 1) * 128],
                           r=xln[:, dc * R + tg * 512: dc * R + (tg + 1) * 512],
                           st=(dc == 0), sp=(dc == NDC - 1):
                        e.matmul(o, l, r, start=st, stop=sp),
                    sem=("pe" if (dc == NDC - 1 and tg == NTG - 1) else None),
                )
        vfree[par] = t_pe
        P.wait("vector", "pe", t_pe)
        P.wait("vector", "st", P.cnt["st"])
        t_v = None
        for tg in range(NTG):
            t_v = P.op("vector", lambda e, o=bstmp[:, tg * 512:(tg + 1) * 512],
                       i=ps[tg][:, :]: e.tensor_copy(o, i), sem="dve")
        P.wait("tensor", "dve", t_v)
        for q4 in range(4):
            tpar = 4 + q4 % 2
            if q4 >= 2:
                P.wait("tensor", "dve", P.cnt["dve"])
            t_t = None
            for i in range(4):
                tt = q4 * 4 + i
                t_t = P.op(
                    "tensor",
                    lambda e, o=ps[tpar][:, :].bitcast(BF16)[:, i * 128:(i + 1) * 128],
                           i_=bstmp[:, tt * 128:(tt + 1) * 128], d=ident[:, :]:
                        e.transpose(o, i_, d),
                    sem=("pe" if i == 3 else None),
                )
            P.wait("vector", "pe", t_t)
            t_c = P.op("vector", lambda e, o=bstmp[:, R + q4 * 512:R + (q4 + 1) * 512],
                       i=ps[tpar][:, :].bitcast(BF16)[:, 0:512]:
                       e.tensor_copy(o, i), sem="dve")
            P.wait("sync", "dve", t_c)
            for i in range(4):
                tt = q4 * 4 + i
                P.dma("sync", VT[ocv, tt],
                      bstmp[:, R + q4 * 512 + i * 128: R + q4 * 512 + (i + 1) * 128],
                      "st")
    P.wait("sync", "st", P.cnt["st"])
    P.barrier()



def emit_P5(P, consts, Bf, io):
    QT, KTp, VTp = io["QT"], io["KT_pair"], io["VT_pair"]
    bcd, mkd, AOT = io["biascols"], io["cmasks"], io["AOT"]
    xln = Bf["xln"]
    scr = Bf["scratch"]
    ksb = xln[:, 0:4096]
    vsb = xln[:, 4096:8192]
    qsb = xln[:, 8192:10240]
    sqs = xln[:, 10240:14336]
    mks = xln[:, 14336:14336 + 4 * 8 * 512]
    pt = [xln[:, 30720:31232], xln[:, 31232:31744], xln[:, 32768:33280]]
    aosb = [xln[:, 31744:32256], xln[:, 32256:32768]]
    ptacc = xln[:, 33280:33792]
    bcs = Bf["bias32"]
    bcs = scr[:, 0:128]
    bcr = scr[:, 128:256]
    rbc = scr[:, 256:768]
    srow = scr[0:1, 768:1280]
    nrm = scr[0:1, 1280:1296]
    nrm2 = scr[0:1, 1296:1312]
    nbc = scr[:, 1312:1313]
    ps = Bf["ps"]
    ps_s = [ps[0], ps[1], ps[5]]
    ps_pv = [ps[2], ps[3]]
    ps_sum = ps[4]
    ps_n = ps[7]
    ps_rbc = ps[6]
    ps_sm = ps[7]

    t_b = P.dma("gpsimd", bcs, bcd[:, :], "xld")
    for j in range(4):
        for ci in range(8):
            P.dma("gpsimd", mks[:, (j * 8 + ci) * 512:(j * 8 + ci + 1) * 512],
                  mkd[j, ci], "xld")
    P.wait_all("gpsimd", "xld")
    P.barrier()

    gmap = {}
    for par, groups in ((0, GROUPS_EVEN), (1, GROUPS_ODD)):
        for slot, g in enumerate(groups):
            gmap[g] = (par, slot)

    def load_kt(hk):
        p2, e = hk // 2, hk % 2
        tgt = None
        for g in range(8):
            rank, slot = gmap[g]
            for half, oc in ((0, 2 * p2), (1, 2 * p2 + 1)):
                tgt = P.dma(
                    "sync", ksb[64 * half:64 * half + 64, 512 * g:512 * (g + 1)],
                    KTp[rank, oc, 64 * e:64 * e + 64, 512 * slot:512 * (slot + 1)],
                    "sld",
                )
        return tgt

    def load_v(hk):
        tgt = None
        for kc in range(32):
            g = kc // 4
            rank, slot = gmap[g]
            tgt = P.dma("sync", vsb[:, kc * 128:(kc + 1) * 128],
                        VTp[rank, hk, slot * 4 + kc % 4], "sld")
        return tgt

    def load_q(h):
        p2, e = h // 2, h % 2
        P.dma("sync", qsb[0:64, :], QT[2 * p2, 64 * e:64 * e + 64, :], "sld")
        return P.dma("sync", qsb[64:128, :], QT[2 * p2 + 1, 64 * e:64 * e + 64, :], "sld")

    def norms2(src, width, out_slot):
        t_a = P.op("scalar", lambda e, o=sqs[:, 0:width], i=src[:, 0:width]:
                   e.activation(o, i, AF.Square), sem="act")
        P.wait("tensor", "act", t_a)
        t_v = None
        for tg in range(width // 512):
            t_pe = P.op(
                "tensor",
                lambda e, o=ps_n[0:1, :], l=consts["ones_bf"][:, :],
                       r=sqs[:, tg * 512:(tg + 1) * 512]:
                    e.matmul(o, l, r, start=True, stop=True),
                sem="pe",
            )
            P.wait("vector", "pe", t_pe)
            t_v = P.op("vector", lambda e, o=nrm[0:1, out_slot + tg:out_slot + tg + 1],
                       i=ps_n[0:1, :]: e.reduce_max(o, i, axis=AX), sem="dve")
            P.wait("tensor", "dve", t_v)
        return t_v

    state = {"aot_free": [0, 0], "h_pe": 0, "srow_free": 0, "rbc_free": 0,
             "prev_done": 0, "prev_acc": 0}
    for hk in range(8):
        if state["h_pe"]:
            P.wait("sync", "pe", state["h_pe"])
        t_k = load_kt(hk)
        t_vv = load_v(hk)
        P.wait_all("scalar", "sld")
        t_nk = norms2(ksb, S, 8)
        P.wait("scalar", "dve", t_nk)
        t_bk = P.op("scalar", lambda e, o=nrm2[0:1, 8:16], i=nrm[0:1, 8:16]:
                    e.copy(o, i), sem="act")
        P.wait("vector", "act", t_bk)
        P.op("vector", lambda e, o=nrm[0:1, 0:1], i=nrm2[0:1, 8:16]:
             e.reduce_max(o, i, axis=AX), sem="dve")
        P.wait_all("tensor", "sld")
        for hq4 in range(4):
            h = 4 * hk + hq4
            if state["h_pe"]:
                P.wait("sync", "pe", state["h_pe"])
            t_q = load_q(h)
            P.wait_all("scalar", "sld")
            t_nq = norms2(qsb, R, 4)
            P.wait("scalar", "dve", t_nq)
            t_bq = P.op("scalar", lambda e, o=nrm2[0:1, 4:8], i=nrm[0:1, 4:8]:
                        e.copy(o, i), sem="act")
            P.wait("vector", "act", t_bq)
            P.op("vector", lambda e, o=nrm[0:1, 1:2], i=nrm2[0:1, 4:8]:
                 e.reduce_max(o, i, axis=AX), sem="dve")
            P.wait("scalar", "dve", P.cnt["dve"])
            t_bb = P.op("scalar", lambda e, o=nrm2[0:1, 0:2], i=nrm[0:1, 0:2]:
                        e.copy(o, i), sem="act")
            P.wait("vector", "act", t_bb)
            P.op("vector", lambda e, o=nrm[0:1, 2:3], a=nrm2[0:1, 0:1], b=nrm2[0:1, 1:2]:
                 e.tensor_tensor(o, a, b, OP.mult), sem="dve")
            P.wait("scalar", "dve", P.cnt["dve"])
            P.op("scalar", lambda e, o=nrm[0:1, 2:3], i=nrm[0:1, 2:3]:
                 e.activation(o, i, AF.Sqrt), sem="act")
            P.wait("vector", "act", P.cnt["act"])
            t_b2 = P.op("vector", lambda e, o=nrm[0:1, 3:4], i=nrm[0:1, 2:3]:
                        e.tensor_scalar_mul(o, i, -ATT_SCALE), sem="dve")
            P.wait("tensor", "dve", t_b2)
            t_pe = P.op("tensor", lambda e, o=ps_sm[:, 0:1], l=consts["ones1_f"][:, :],
                        r=nrm[0:1, 3:4]: e.matmul(o, l, r, start=True, stop=True),
                        sem="pe")
            P.wait("scalar", "pe", t_pe)
            t_nbc = P.op("scalar", lambda e, o=nbc, i=ps_sm[:, 0:1]:
                         e.copy(o, i), sem="act")
            P.wait("vector", "act", t_nbc)
            t_bc = P.op("vector", lambda e, o=bcr, i=bcs, s=nbc:
                        e.tensor_scalar(o, i, s, None, OP.add), sem="dve")
            P.wait("scalar", "dve", t_bc)
            for j in range(4):
                jpar = j % 2
                if state["aot_free"][jpar]:
                    P.wait("tensor", "dve", state["aot_free"][jpar])
                nkc = BOUND[j]

                pv_cnt = {}
                acc_cnt = {}

                def emit_S(kc, j=j, pv_cnt=pv_cnt, acc_cnt=acc_cnt):
                    spar = kc % 3
                    masked = kc in CAND[j]
                    if masked:
                        ci = kc - CAND[j][0]
                        P.op(
                            "tensor",
                            lambda e, o=ps_s[spar][:, :], l=Bf["ident_bf"][:, :],
                                   r=mks[:, (j * 8 + ci) * 512:(j * 8 + ci + 1) * 512]:
                                e.matmul(o, l, r, start=True, stop=False),
                        )
                    t_s = P.op(
                        "tensor",
                        lambda e, o=ps_s[spar][:, :], l=ksb[:, kc * 128:(kc + 1) * 128],
                               r=qsb[:, j * 512:(j + 1) * 512],
                               st=(not masked):
                            e.matmul(o, l, r, start=st, stop=True),
                        sem="pe",
                    )
                    P.wait("scalar", "pe", t_s)
                    g = pv_cnt.get(kc - 3, state["prev_done"])
                    if g:
                        P.wait("scalar", "pe", g)
                    ga = acc_cnt.get(kc - 3, state["prev_acc"])
                    if ga:
                        P.wait("scalar", "dve", ga)
                    t_e = P.op(
                        "scalar",
                        lambda e, o=pt[spar], i=ps_s[spar][:, :],
                               b=bcr[:, j * 32 + kc:j * 32 + kc + 1]:
                            e.activation(o, i, AF.Exp, bias=b, scale=ATT_SCALE),
                        sem="act",
                    )
                    return t_e

                t_e = {}
                for kk in range(min(2, nkc)):
                    t_e[kk] = emit_S(kk)
                t_q4 = 0
                for kc in range(nkc):
                    spar = kc % 3
                    if kc + 2 < nkc:
                        t_e[kc + 2] = emit_S(kc + 2)
                    t_ek = t_e.pop(kc)
                    P.wait("tensor", "act", t_ek)
                    P.wait("vector", "act", t_ek)
                    if kc % 4 == 0:
                        if t_q4:
                            P.wait("vector", "pe", t_q4)
                        t_pa = P.op("vector", lambda e, o=ptacc, i=pt[spar]:
                                    e.tensor_copy(o, i), sem="dve")
                    else:
                        t_pa = P.op("vector", lambda e, o=ptacc, a=ptacc, b=pt[spar]:
                                    e.tensor_tensor(o, a, b, OP.add), sem="dve")
                    acc_cnt[kc] = t_pa
                    if kc % 4 == 3 or kc == nkc - 1:
                        P.wait("tensor", "dve", t_pa)
                        t_q4 = P.op(
                            "tensor",
                            lambda e, o=ps_sum[0:1, :], l=consts["ones_bf"][:, :],
                                   r=ptacc, st=(kc < 4), sp=(kc == nkc - 1):
                                e.matmul(o, l, r, start=st, stop=sp),
                            sem="pe",
                        )
                    pv_cnt[kc] = P.op(
                        "tensor",
                        lambda e, o=ps_pv[jpar][:, :], l=vsb[:, kc * 128:(kc + 1) * 128],
                               r=pt[spar], st=(kc == 0), sp=(kc == nkc - 1):
                            e.matmul(o, l, r, start=st, stop=sp),
                        sem="pe",
                    )
                t_done = P.cnt["pe"]
                state["prev_done"] = t_done
                state["prev_acc"] = acc_cnt[nkc - 1]
                P.wait("scalar", "pe", t_done)
                if state["srow_free"]:
                    P.wait("scalar", "dve", state["srow_free"])
                t_cs = P.op("scalar", lambda e, o=srow, i=ps_sum[0:1, :]:
                            e.copy(o, i), sem="act")
                P.wait("vector", "act", t_cs)
                t_rc = P.op("vector", lambda e, o=srow, i=srow:
                            e.reciprocal(o, i), sem="dve")
                state["srow_free"] = t_rc
                P.wait("tensor", "dve", t_rc)
                t_pe2 = P.op("tensor", lambda e, o=ps_rbc[:, :], l=consts["ones1_f"][:, :],
                             r=srow: e.matmul(o, l, r, start=True, stop=True),
                             sem="pe")
                P.wait("scalar", "pe", t_pe2)
                if state["rbc_free"]:
                    P.wait("scalar", "dve", state["rbc_free"])
                t_rbc = P.op("scalar", lambda e, o=rbc, i=ps_rbc[:, :]:
                             e.copy(o, i), sem="act")
                P.wait("vector", "act", t_rbc)
                if state.setdefault("ao_st", [0, 0])[jpar]:
                    P.wait_all("vector", "st")
                t_ao = P.op("vector", lambda e, o=aosb[jpar], a=ps_pv[jpar][:, :], b=rbc:
                            e.tensor_tensor(o, a, b, OP.mult), sem="dve")
                state["rbc_free"] = t_ao
                state["aot_free"][jpar] = t_ao
                P.wait("sync", "dve", t_ao)
                state["ao_st"][jpar] = P.dma(
                    "sync", AOT[h, :, j * 512:(j + 1) * 512], aosb[jpar], "st")
            state["h_pe"] = P.cnt["pe"]
    P.wait("sync", "st", P.cnt["st"])
    P.barrier()



def emit_P6(P, consts, Bf, io):
    xT, AOTd, wo, xo = io["xT2"], io["AOT"], io["wo_t"], io["xT3"]
    aot = Bf["xln"]
    t_l = None
    for hc in range(32):
        t_l = P.dma("gpsimd", aot[:, hc * R:(hc + 1) * R], AOTd[hc], "xld")
    P.wait("tensor", "xld", t_l)
    P.barrier()

    xres = Bf["xres"]
    state = {"out_tgt": [0, 0], "st_x": [0, 0]}

    def epilogue(oc, grp):
        par = oc % 2
        if state["out_tgt"][par]:
            P.wait("sync", "dve", state["out_tgt"][par])
        if state["st_x"][par]:
            P.wait_all("sync", "st")
        t_x = P.dma("sync", xres[par][:, :], xT[oc * 128:(oc + 1) * 128, :], "sld")
        t_pe = P.cnt["pe"]
        P.wait("vector", "pe", t_pe)
        P.wait("vector", "sld", t_x)
        t_v = None
        for tg in range(NTG):
            t_v = P.op(
                "vector",
                lambda e, o=xres[par][:, tg * 512:(tg + 1) * 512], a=grp[tg][:, :],
                       b=xres[par][:, tg * 512:(tg + 1) * 512]:
                    e.tensor_tensor(o, a, b, OP.add),
                sem="dve",
            )
        state["out_tgt"][par] = t_v
        P.wait("sync", "dve", t_v)
        state["st_x"][par] = P.dma(
            "sync", xo[oc * 128:(oc + 1) * 128, :], xres[par][:, :], "st")
        return t_v

    emit_ws_loop(P, 32, lambda oc: wo[oc], Bf, epilogue)
    P.wait("sync", "st", P.cnt["st"])
    P.barrier()



def emit_P7(P, consts, Bf, io):
    xT, W0T, ATd = io["xT3"], io["W0T_t"], io["AT_r"]
    BTd, b0d, bit0d, bit1d, xo = io["BT"], io["b0"], io["bit0"], io["bit1"], io["xT4"]
    scr = Bf["scratch"]
    bstmp = Bf["bstmp"]
    xln = Bf["xln"]
    ps = Bf["ps"]
    ident = Bf["ident_bf"]
    b0c = Bf["bias32"]
    bt = Bf["bt"]
    bits = Bf["bits"]
    at_sb = Bf["r_bc"][:, :]
    hT = [Bf["m_bc"][0:16, :], Bf["m_bc"][32:48, :], Bf["m_bc"][64:80, :],
          Bf["hT3"][:, :]]
    ysb = bstmp[:, R:2 * R]

    P.dma("gpsimd", b0c[:, :], b0d.rearrange("(c p) -> p c", p=128), "xld",
          allow_slow_non_contiguous=True)
    emit_stats_xln(P, xT, Bf, consts, nsq=1)
    t_at = P.dma("gpsimd", at_sb, ATd[:, :], "xld")

    P.wait("tensor", "xld", t_at)
    for tt in range(NTT):
        par = tt % 2
        t_pe = None
        for dc in range(NDC):
            t_pe = P.op(
                "tensor",
                lambda e, o=ps[par][:, 0:64],
                       l=xln[:, dc * R + tt * 128: dc * R + (tt + 1) * 128],
                       r=at_sb[:, dc * 64:(dc + 1) * 64],
                       st=(dc == 0), sp=(dc == NDC - 1):
                    e.matmul(o, l, r, start=st, stop=sp),
                sem=("pe" if dc == NDC - 1 else None),
            )
        P.wait("vector", "pe", t_pe)
        t_c = P.op("vector", lambda e, o=bstmp[:, 0:64], i=ps[par][:, 0:64]:
                   e.tensor_copy(o, i), sem="dve")
        P.wait("tensor", "dve", t_c)
        for l in range(4):
            t_t = P.op(
                "tensor",
                lambda e, o=ps[2][:, :].bitcast(BF16)[0:16, 0:128],
                       i=bstmp[:, l * 16:(l + 1) * 16], d=ident[:, :]:
                    e.transpose(o, i, d),
                sem="pe",
            )
            P.wait("vector", "pe", t_t)
            t_cc = P.op("vector", lambda e, o=hT[l][:, tt * 128:(tt + 1) * 128],
                        i=ps[2][:, :].bitcast(BF16)[0:16, 0:128]:
                        e.tensor_copy(o, i), sem="dve")
            P.wait("tensor", "dve", t_cc)
    P.barrier()

    xres = Bf["xres"]
    state = {"out_tgt": [0, 0], "st_x": [0, 0], "y_pe": [0, 0]}

    def extra_loads(oc):
        par = oc % 2
        if state["y_pe"][par]:
            P.wait("gpsimd", "pe", state["y_pe"][par])
        for l in range(4):
            P.dma("gpsimd", bt[par][l][:, :], BTd[l, :, oc * 128:(oc + 1) * 128], "wld")
        if oc > 0:
            P.wait("sync", "dve", state["out_tgt"][(oc - 1) % 2])
        P.dma("sync", bits[0][:, :], bit0d[oc * 128:(oc + 1) * 128, :], "sld")
        t = P.dma("sync", bits[1][:, :], bit1d[oc * 128:(oc + 1) * 128, :], "sld")
        P.wait("tensor", "wld", P.cnt["wld"])
        P.wait("vector", "sld", t)

    def epilogue(oc, grp):
        par = oc % 2
        if state["out_tgt"][par]:
            P.wait("sync", "dve", state["out_tgt"][par])
        if state["st_x"][par]:
            P.wait_all("sync", "st")
        t_x = P.dma("sync", xres[par][:, :], xT[oc * 128:(oc + 1) * 128, :], "sld")
        t_v = None
        for tg in range(NTG):
            if tg > 0:
                P.wait("tensor", "dve", P.cnt["dve"])
            t_y = None
            for l in range(4):
                t_y = P.op(
                    "tensor",
                    lambda e, o=ps[4 + l][:, :], lh=bt[par][l][:, :],
                           r=hT[l][:, tg * 512:(tg + 1) * 512]:
                        e.matmul(o, lh, r, start=True, stop=True),
                    sem=("pe" if l == 3 else None),
                )
            P.wait("scalar", "pe", t_y)
            state["y_pe"][par] = t_y
            bsl = slice(tg * 512, (tg + 1) * 512)
            P.op("scalar", lambda e, o=ysb[:, 0:512], i=ps[4][:, :]: e.copy(o, i))
            P.op("scalar", lambda e, o=ysb[:, 512:1024], i=ps[5][:, :]: e.copy(o, i))
            P.op("scalar", lambda e, o=ysb[:, 1024:1536], i=ps[6][:, :]: e.copy(o, i))
            t_y3 = P.op("scalar", lambda e, o=ysb[:, 1536:2048], i=ps[7][:, :]:
                        e.copy(o, i), sem="act")
            P.wait("vector", "act", t_y3)
            P.op("vector", lambda e, o=scr[:, 1024:1536], m=bits[0][:, bsl],
                 a=ysb[:, 512:1024], b=ysb[:, 0:512]: e.select(o, m, a, b))
            P.op("vector", lambda e, o=scr[:, 1536:2048], m=bits[0][:, bsl],
                 a=ysb[:, 1536:2048], b=ysb[:, 1024:1536]: e.select(o, m, a, b))
            P.op("vector", lambda e, o=scr[:, 0:512], m=bits[1][:, bsl],
                 a=scr[:, 1536:2048], b=scr[:, 1024:1536]: e.select(o, m, a, b))
            P.op("vector", lambda e, o=scr[:, 512:1024], a=scr[:, 0:512], b=grp[tg][:, :]:
                 e.scalar_tensor_tensor(o, a, 8.0, b, OP.mult, OP.add))
            if tg == 0:
                P.wait("vector", "sld", t_x)
            t_v = P.op(
                "vector",
                lambda e, o=xres[par][:, bsl], a=scr[:, 512:1024],
                       s=b0c[:, oc:oc + 1], b=xres[par][:, bsl]:
                    e.scalar_tensor_tensor(o, a, s, b, OP.add, OP.add),
                sem="dve",
            )
        state["out_tgt"][par] = t_v
        P.wait("sync", "dve", t_v)
        state["st_x"][par] = P.dma(
            "sync", xo[oc * 128:(oc + 1) * 128, :], xres[par][:, :], "st")
        return t_v

    emit_ws_loop(P, 32, lambda oc: W0T[oc], Bf, epilogue, ping_pong=False,
                 extra_pe_wait=extra_loads)
    P.wait("sync", "st", P.cnt["st"])
    P.barrier()



def emit_P8(P, consts, Bf, io):
    xT, out = io["xT4"], io["out"]
    scr = Bf["scratch"]
    ident = Bf["ident_f"]
    tin = [scr[:, 0:512], scr[:, 512:1024]]
    tout = [scr[:, 1024:1536], scr[:, 1536:2048]]
    ps = Bf["ps"]

    state = {"tin_free": [0, 0], "tout_free": [0, 0], "ps_free": [0, 0]}
    for tt in range(NTT):
        for d4 in range(8):
            par = d4 % 2
            if state["tin_free"][par]:
                P.wait("gpsimd", "pe", state["tin_free"][par])
            if state["tout_free"][par]:
                P.wait_all("vector", "st")
            t_l = None
            for i in range(4):
                dc = d4 * 4 + i
                t_l = P.dma("gpsimd", tin[par][:, i * 128:(i + 1) * 128],
                            xT[dc * 128:(dc + 1) * 128, tt * 128:(tt + 1) * 128],
                            "xld")
            P.wait("tensor", "xld", t_l)
            if state["ps_free"][par]:
                P.wait("tensor", "dve", state["ps_free"][par])
            t_t = None
            for i in range(4):
                t_t = P.op(
                    "tensor",
                    lambda e, o=ps[par][:, i * 128:(i + 1) * 128],
                           i_=tin[par][:, i * 128:(i + 1) * 128], d=ident[:, :]:
                        e.matmul(o, i_, d, start=True, stop=True),
                    sem=("pe" if i == 3 else None),
                )
            state["tin_free"][par] = t_t
            P.wait("vector", "pe", t_t)
            t_cp = P.op("vector", lambda e, o=tout[par], i=ps[par][:, :]:
                        e.tensor_copy(o, i), sem="dve")
            state["ps_free"][par] = t_cp
            P.wait("sync", "dve", t_cp)
            state["tout_free"][par] = P.dma(
                "sync", out[tt * 128:(tt + 1) * 128, d4 * 512:(d4 + 1) * 512],
                tout[par], "st")
    P.wait("sync", "st", P.cnt["st"])



def _ext(nc, name, shape, dt, out=False):
    kind = "ExternalOutput" if out else "ExternalInput"
    return nc.dram_tensor(name, shape, dt, kind=kind).ap()


IN_SPECS = {
    "xT": ([D, R], F32), "w1_t": ([32, 128, 32 * 128], F32),
    "w2T_r": ([128, 32 * 512], F32), "mask_s": ([4096, 512], F32),
    "b2": ([D], F32), "WT_t": ([32, 128, 32 * 128], F32),
    "gwT_r": ([128, 32 * 8], F32), "ics": ([8, D], F32), "ocs": ([8, D], F32),
    "ffn_bias": ([D], F32), "wqkv_t": ([48, 128, 32 * 128], F32),
    "cos2": ([128, R], F32), "sin2": ([128, R], F32),
    "biascols": ([128, 128], F32), "cmasks": ([4, 8, 128, 512], BF16),
    "wo_t": ([32, 128, 32 * 128], F32), "W0T_t": ([32, 128, 32 * 128], F32),
    "AT_r": ([128, 32 * 64], F32), "BT": ([4, 16, D], F32), "b0": ([D], F32),
    "bit0": ([D, R], U8), "bit1": ([D, R], U8),
}


def build_fused():
    P = Prog()
    nc = P.nc
    io = {k: _ext(nc, k, sh, dt) for k, (sh, dt) in IN_SPECS.items()}
    io["out"] = _ext(nc, "out", [R, D], F32, out=True)
    io["attn_t"] = nc.dram_tensor("attn_t", [4, 128, 32, 128], BF16).ap()
    io["attn_all"] = nc.dram_tensor("attn_all", [32, 128, 32 * 128], BF16,
                                    addr_space="Shared").ap()
    fdbg = os.environ.get("FDBG") == "1"
    for n in ("xT1", "xT2", "xT3", "xT4"):
        io[n] = nc.dram_tensor(n, [D, R], F32,
                               kind=("ExternalOutput" if fdbg else "Internal")).ap()
    if fdbg:
        io["AOT"] = nc.dram_tensor("AOT", [32, 128, R], BF16, kind="ExternalOutput").ap()
    io["QT"] = nc.dram_tensor("QT", [32, 128, R], BF16).ap()
    io["KT"] = nc.dram_tensor("KT", [8, 128, R], BF16).ap()
    io["Vt"] = nc.dram_tensor("Vt", [8, 16, 128, 128], BF16).ap()
    io["KT_pair"] = nc.dram_tensor("KT_pair", [2, 8, 128, R], BF16).ap()
    io["VT_pair"] = nc.dram_tensor("VT_pair", [2, 8, 16, 128, 128], BF16).ap()
    if not fdbg:
        io["AOT"] = nc.dram_tensor("AOT", [32, 128, R], BF16).ap()

    consts = setup_common(P)
    Bf = alloc_shared(P)
    emit_identity(P, Bf["ident_bf"])
    emit_identity(P, Bf["ident_f"])

    emit_P1(P, consts, Bf, io)
    t_cc = P.op("gpsimd", lambda e, i=io["attn_t"], o=io["attn_all"]:
                e.collective_compute(
                    "AllGather", OP.bypass, replica_groups=[list(range(8))],
                    ins=[i[:, :, :, :].rearrange("a b c d -> (a b c) d")],
                    outs=[o[:, :, :].rearrange("a b (c d) -> (a b c) d", d=128)]),
                sem="cc")
    for e in ("tensor", "vector", "scalar", "sync", "gpsimd"):
        P.wait(e, "cc", t_cc)

    emit_P2(P, consts, Bf, io)
    emit_P3(P, consts, Bf, io)
    emit_P4(P, consts, Bf, io)
    pairs = [[0, 1], [2, 3], [4, 5], [6, 7]]
    t_cc = P.op("gpsimd", lambda e, i=io["KT"], o=io["KT_pair"]:
                e.collective_compute(
                    "AllGather", OP.bypass, replica_groups=pairs,
                    ins=[i[:, :, :].rearrange("a b c -> (a b) c")],
                    outs=[o[:, :, :, :].rearrange("a b c d -> (a b c) d")]),
                sem="cc")
    t_cc = P.op("gpsimd", lambda e, i=io["Vt"], o=io["VT_pair"]:
                e.collective_compute(
                    "AllGather", OP.bypass, replica_groups=pairs,
                    ins=[i[:, :, :, :].rearrange("a b c d -> (a b c) d")],
                    outs=[o[:, :, :, :, :].rearrange("a b c d f -> (a b c d) f")]),
                sem="cc")
    for e in ("tensor", "vector", "scalar", "sync", "gpsimd"):
        P.wait(e, "cc", t_cc)

    emit_P5(P, consts, Bf, io)
    emit_P6(P, consts, Bf, io)
    emit_P7(P, consts, Bf, io)
    emit_P8(P, consts, Bf, io)
    return P.finalize()


def _standalone(names, outs, emit, pre_collective=None):
    def build():
        P = Prog()
        nc = P.nc
        io = {}
        for n in names:
            io[n] = _ext(nc, n, *IN_SPECS[n]) if n in IN_SPECS else None
        for n, (sh, dt) in outs.items():
            io[n] = _ext(nc, n, sh, dt, out=True)
        consts = setup_common(P)
        Bf = alloc_shared(P)
        emit_identity(P, Bf["ident_bf"])
        emit_identity(P, Bf["ident_f"])
        emit(P, consts, Bf, io)
        return P.finalize()
    return build


def build_P1():
    P = Prog(); nc = P.nc
    io = {n: _ext(nc, n, *IN_SPECS[n]) for n in ("w1_t", "w2T_r", "mask_s")}
    io["attn_t"] = _ext(nc, "attn_t", [4, 128, 32, 128], BF16, out=True)
    consts = setup_common(P); Bf = alloc_shared(P)
    emit_P1(P, consts, Bf, io)
    return P.finalize()


def build_P2():
    P = Prog(); nc = P.nc
    io = {n: _ext(nc, n, *IN_SPECS[n]) for n in ("xT", "b2")}
    io["attn_all"] = _ext(nc, "attn_all", [32, 128, 32 * 128], BF16)
    io["xT1"] = _ext(nc, "xT1", [D, R], F32, out=True)
    consts = setup_common(P); Bf = alloc_shared(P)
    emit_P2(P, consts, Bf, io)
    return P.finalize()


def build_P3():
    P = Prog(); nc = P.nc
    io = {n: _ext(nc, n, *IN_SPECS[n]) for n in ("WT_t", "gwT_r", "ics", "ocs", "ffn_bias")}
    io["xT1"] = _ext(nc, "xT1", [D, R], F32)
    io["xT2"] = _ext(nc, "xT2", [D, R], F32, out=True)
    consts = setup_common(P); Bf = alloc_shared(P)
    emit_identity(P, Bf["ident_bf"])
    emit_P3(P, consts, Bf, io)
    return P.finalize()


def build_P4():
    P = Prog(); nc = P.nc
    io = {n: _ext(nc, n, *IN_SPECS[n]) for n in ("wqkv_t", "cos2", "sin2")}
    io["xT2"] = _ext(nc, "xT2", [D, R], F32)
    io["QT"] = _ext(nc, "QT", [32, 128, R], BF16, out=True)
    io["KT"] = _ext(nc, "KT", [8, 128, R], BF16, out=True)
    io["Vt"] = _ext(nc, "Vt", [8, 16, 128, 128], BF16, out=True)
    consts = setup_common(P); Bf = alloc_shared(P)
    emit_identity(P, Bf["ident_bf"])
    emit_P4(P, consts, Bf, io)
    return P.finalize()


def build_P5():
    P = Prog(); nc = P.nc
    io = {n: _ext(nc, n, *IN_SPECS[n]) for n in ("biascols", "cmasks")}
    io["QT"] = _ext(nc, "QT", [32, 128, R], BF16)
    io["KT_pair"] = _ext(nc, "KT_pair", [2, 8, 128, R], BF16)
    io["VT_pair"] = _ext(nc, "VT_pair", [2, 8, 16, 128, 128], BF16)
    io["AOT"] = _ext(nc, "AOT", [32, 128, R], BF16, out=True)
    consts = setup_common(P); Bf = alloc_shared(P)
    emit_P5(P, consts, Bf, io)
    return P.finalize()


def build_P6():
    P = Prog(); nc = P.nc
    io = {n: _ext(nc, n, *IN_SPECS[n]) for n in ("wo_t",)}
    io["xT2"] = _ext(nc, "xT2", [D, R], F32)
    io["AOT"] = _ext(nc, "AOT", [32, 128, R], BF16)
    io["xT3"] = _ext(nc, "xT3", [D, R], F32, out=True)
    consts = setup_common(P); Bf = alloc_shared(P)
    emit_P6(P, consts, Bf, io)
    return P.finalize()


def build_P7():
    P = Prog(); nc = P.nc
    io = {n: _ext(nc, n, *IN_SPECS[n]) for n in ("W0T_t", "AT_r", "BT", "b0", "bit0", "bit1")}
    io["xT3"] = _ext(nc, "xT3", [D, R], F32)
    io["xT4"] = _ext(nc, "xT4", [D, R], F32, out=True)
    consts = setup_common(P); Bf = alloc_shared(P)
    emit_identity(P, Bf["ident_bf"])
    emit_P7(P, consts, Bf, io)
    return P.finalize()


def build_P8():
    P = Prog(); nc = P.nc
    io = {}
    io["xT4"] = _ext(nc, "xT4", [D, R], F32)
    io["out"] = _ext(nc, "out", [R, D], F32, out=True)
    consts = setup_common(P); Bf = alloc_shared(P)
    emit_identity(P, Bf["ident_f"])
    emit_P8(P, consts, Bf, io)
    return P.finalize()



def host_prepare(inputs):
    H = {}
    x = np.asarray(inputs["L0_attn_x"], np.float32).reshape(B * S, D)
    H["xT"] = [np.ascontiguousarray(x[rows_for_core(c)].T) for c in range(NCORES)]
    H["w1_t"] = tile_lhsT(np.asarray(inputs["L0_attn_w1"], np.float32)).reshape(32, 128, -1)
    w2T = np.ascontiguousarray(np.asarray(inputs["L0_attn_w2"], np.float32).T)
    mask = np.asarray(inputs["L0_attn_sparse_mask"], np.float32)
    H["w2T_r"] = [
        tile_rhs(np.ascontiguousarray(w2T[:, c * 512:(c + 1) * 512]), 512).reshape(128, -1)
        for c in range(NCORES)
    ]
    H["mask_s"] = [np.ascontiguousarray(mask[:, c * 512:(c + 1) * 512])
                   for c in range(NCORES)]
    H["b2"] = np.asarray(inputs["L0_attn_b2"], np.float32)
    H["WT_t"] = tile_lhsT(np.ascontiguousarray(
        np.asarray(inputs["L0_ffn_weight"], np.float32).T)).reshape(32, 128, -1)
    H["gwT_r"] = tile_rhs(np.ascontiguousarray(
        np.asarray(inputs["L0_ffn_gate_w"], np.float32).T), 8).reshape(128, -1)
    H["ics"] = np.asarray(inputs["L0_ffn_in_channel_scale"], np.float32)
    H["ocs"] = np.asarray(inputs["L0_ffn_out_channel_scale"], np.float32)
    H["ffn_bias"] = np.asarray(inputs["L0_ffn_bias"], np.float32)
    wq = np.asarray(inputs["L1_attn_w_qkv"], np.float32)[:, qkv_perm()]
    H["wqkv_t"] = tile_lhsT(np.ascontiguousarray(wq)).reshape(48, 128, -1)
    H["wo_t"] = tile_lhsT(np.asarray(inputs["L1_attn_w_o"], np.float32)).reshape(32, 128, -1)
    H["W0T_t"] = tile_lhsT(np.ascontiguousarray(
        np.asarray(inputs["L1_ffn_W0"], np.float32).T)).reshape(32, 128, -1)
    A = np.asarray(inputs["L1_ffn_A"], np.float32)
    H["AT_r"] = tile_rhs(np.ascontiguousarray(A.transpose(2, 0, 1).reshape(D, 64)),
                         64).reshape(128, -1)
    H["BT"] = np.ascontiguousarray(np.asarray(inputs["L1_ffn_B"], np.float32)
                                   .transpose(0, 2, 1))
    H["b0"] = np.asarray(inputs["L1_ffn_b0"], np.float32)
    ropes = [rope_tables(c) for c in range(NCORES)]
    H["cos2"] = [r[0] for r in ropes]
    H["sin2"] = [r[1] for r in ropes]
    cd = [causal_data(c) for c in range(NCORES)]
    H["biascols"] = [c[0] for c in cd]
    H["cmasks"] = [c[1] for c in cd]
    bit0, bit1 = dropout_bits()
    bit0 = bit0.reshape(B * S, D)
    bit1 = bit1.reshape(B * S, D)
    H["bit0"] = [np.ascontiguousarray(bit0[rows_for_core(c)].T) for c in range(NCORES)]
    H["bit1"] = [np.ascontiguousarray(bit1[rows_for_core(c)].T) for c in range(NCORES)]
    return H


def run(nc, in_maps):
    return run_bass_kernel_spmd(nc, in_maps, core_ids=list(range(NCORES))).results


PER_CORE = {"xT", "w2T_r", "mask_s", "cos2", "sin2", "biascols", "cmasks",
            "bit0", "bit1"}


def kernel(**inputs):
    H = host_prepare(inputs)
    nc = build_fused()
    in_maps = [
        {k: (H[k][c] if k in PER_CORE else H[k]) for k in IN_SPECS}
        for c in range(NCORES)
    ]
    res = run(nc, in_maps)
    out = np.empty((B * S, D), np.float32)
    for c in range(NCORES):
        out[rows_for_core(c)] = res[c]["out"]
    return out.reshape(B, S, D)


def kernel_unfused(**inputs):
    H = host_prepare(inputs)
    r1 = run(build_P1(), [
        {"w1_t": H["w1_t"], "w2T_r": H["w2T_r"][c], "mask_s": H["mask_s"][c]}
        for c in range(NCORES)
    ])
    attn_all = np.ascontiguousarray(
        np.concatenate([np.asarray(r1[c]["attn_t"]) for c in range(NCORES)],
                       axis=0)).reshape(32, 128, 32 * 128)
    r2 = run(build_P2(), [
        {"xT": H["xT"][c], "attn_all": attn_all, "b2": H["b2"]}
        for c in range(NCORES)
    ])
    r3 = run(build_P3(), [
        {"xT1": r2[c]["xT1"], "WT_t": H["WT_t"], "gwT_r": H["gwT_r"],
         "ics": H["ics"], "ocs": H["ocs"], "ffn_bias": H["ffn_bias"]}
        for c in range(NCORES)
    ])
    r4 = run(build_P4(), [
        {"xT2": r3[c]["xT2"], "wqkv_t": H["wqkv_t"],
         "cos2": H["cos2"][c], "sin2": H["sin2"][c]}
        for c in range(NCORES)
    ])
    r5 = run(build_P5(), [
        {"QT": r4[c]["QT"],
         "KT_pair": np.stack([r4[c // 2 * 2]["KT"], r4[c // 2 * 2 + 1]["KT"]]),
         "VT_pair": np.stack([r4[c // 2 * 2]["Vt"], r4[c // 2 * 2 + 1]["Vt"]]),
         "biascols": H["biascols"][c], "cmasks": H["cmasks"][c]}
        for c in range(NCORES)
    ])
    r6 = run(build_P6(), [
        {"xT2": r3[c]["xT2"], "AOT": r5[c]["AOT"], "wo_t": H["wo_t"]}
        for c in range(NCORES)
    ])
    r7 = run(build_P7(), [
        {"xT3": r6[c]["xT3"], "W0T_t": H["W0T_t"], "AT_r": H["AT_r"],
         "BT": H["BT"], "b0": H["b0"], "bit0": H["bit0"][c], "bit1": H["bit1"][c]}
        for c in range(NCORES)
    ])
    r8 = run(build_P8(), [{"xT4": r7[c]["xT4"]} for c in range(NCORES)])
    out = np.empty((B * S, D), np.float32)
    for c in range(NCORES):
        out[rows_for_core(c)] = r8[c]["out"]
    return out.reshape(B, S, D)
